# version 37
# baseline (speedup 1.0000x reference)
import contextlib
import numpy as np
import ml_dtypes

import concourse.bass as bass
import concourse.mybir as mybir
from concourse.bass_utils import run_bass_kernel_spmd

F32 = mybir.dt.float32
BF16 = mybir.dt.bfloat16
AF = mybir.ActivationFunctionType
ALU = mybir.AluOpType
AX = mybir.AxisListType
bfd = ml_dtypes.bfloat16

NCORES = 8
TOK = 2048
HT = 1024
NT = 8
NA = 9
E = 1024
NH = 16
D = 64
K17 = 17
SCALE = float(D) ** -0.5
PEN = 10000.0
NEXP = 8
FEXP = 512
EPS = 1e-5
XROWS = 2176
CPT = NH * K17
WIDE = NH * 144


class Op:
    __slots__ = ("i", "eng", "emit", "rd", "wr", "dma", "tag",
                 "deps", "inc", "cnt", "marker", "slot", "prev_target",
                 "strips")

    def __init__(self, i, eng, emit, rd, wr, dma, tag):
        self.i, self.eng, self.emit = i, eng, emit
        self.rd, self.wr, self.dma, self.tag = rd, wr, dma, tag
        self.deps = []
        self.inc = False
        self.cnt = 0
        self.marker = None
        self.strips = None


def _boxes(aps):
    out = []
    for ap in aps:
        if ap is None or not isinstance(ap, bass.AP):
            continue
        space = str(ap.space)
        name = ap.name
        off = int(ap.offset)
        isz = mybir.dt.size(ap.dtype)
        dims = list(ap.ap)
        if "PSUM" in space.upper() or "Psum" in space or "PSum" in space:
            p_lo = ap.base_partition()
            p_hi = p_lo + dims[0][1]
            pstep = dims[0][0]
            foff = off - p_lo * pstep
            fspan = 1 + sum(abs(s) * (n - 1) for s, n in dims[1:])
            b_lo = (foff * isz // 2048) * 2048
            b_hi = ((foff + fspan) * isz + 2047) // 2048 * 2048
            out.append((name, 0, 128, b_lo, b_hi))
        elif "SB" in space.upper():
            p_lo = ap.base_partition()
            p_hi = p_lo + dims[0][1]
            pstep = dims[0][0]
            foff = off - p_lo * pstep
            fspan = 1 + sum(abs(s) * (n - 1) for s, n in dims[1:])
            out.append((name, p_lo, p_hi, foff * isz, (foff + fspan) * isz))
        else:
            span = 1 + sum(abs(s) * (n - 1) for s, n in dims)
            out.append((name, 0, 128, off * isz, (off + span) * isz))
    return out


def _overlap(a, b):
    return (a[1] < b[2] and b[1] < a[2] and a[3] < b[4] and b[3] < a[4])


class Prog:
    def __init__(self):
        self.ops = []
        self.w_by_name = {}
        self.r_by_name = {}

    def op(self, eng, emit, rd=(), wr=(), dma=False, tag="", strips=None):
        o = Op(len(self.ops), eng, emit, _boxes(rd), _boxes(wr), dma, tag)
        o.strips = strips
        deps = set()
        for box in o.rd:
            for wbox, wi in self.w_by_name.get(box[0], ()):
                if _overlap(box, wbox):
                    deps.add(wi)
        for box in o.wr:
            for wbox, wi in self.w_by_name.get(box[0], ()):
                if _overlap(box, wbox):
                    deps.add(wi)
            for rbox, ri in self.r_by_name.get(box[0], ()):
                if _overlap(box, rbox):
                    deps.add(ri)
        o.deps = sorted(deps)
        self.ops.append(o)
        for box in o.wr:
            self.w_by_name.setdefault(box[0], []).append((box, o.i))
        for box in o.rd:
            self.r_by_name.setdefault(box[0], []).append((box, o.i))
        return o

    NSLOT = 8

    def build(self, block, sems):
        import os as _os
        ops = self.ops
        if _os.environ.get("KSERIAL"):
            for o in ops:
                if o.i > 0:
                    o.deps = [o.i - 1]

        rr = {}
        slot_target = {}
        for o in ops:
            if o.dma:
                q = "q_" + o.eng
                s = rr.get(q, 0)
                rr[q] = (s + 1) % Prog.NSLOT
                o.slot = f"{q}_{s}"
                o.prev_target = slot_target.get(o.slot, 0)
                slot_target[o.slot] = o.prev_target + 16
                o.cnt = slot_target[o.slot]

        def chan(o):
            return o.slot if o.dma else o.eng

        def need_edge(p, o):
            if chan(p) != chan(o):
                return True
            if p.dma or o.dma:
                return True
            if o.eng != "tensor":
                return True
            a, b = p.strips, o.strips
            if a is None or b is None:
                return True
            return a[1] <= b[0] or b[1] <= a[0]

        for o in ops:
            if o.dma:
                o.inc = True
            for d in o.deps:
                p = ops[d]
                if need_edge(p, o):
                    p.inc = True
        counts = {}
        for o in ops:
            if o.dma:
                continue
            c = o.eng
            if o.inc:
                counts[c] = counts.get(c, 0) + 1
                o.cnt = counts[c]
        nxt = {}
        for o in reversed(ops):
            c = chan(o)
            if o.inc:
                nxt[c] = o
            o.marker = nxt.get(c)
        totals = dict(counts)
        totals.update(slot_target)

        def runner(eng_name):
            floors = {}

            def _run(eng):
                for o in ops:
                    if o.eng != eng_name:
                        continue
                    need = {}
                    for d in o.deps:
                        p = ops[d]
                        pc = chan(p)
                        if not need_edge(p, o):
                            continue
                        m = p if p.inc else p.marker
                        assert m is not None, f"no marker for op {p.i} ({p.tag})"
                        v = m.cnt
                        if need.get(pc, 0) < v:
                            need[pc] = v
                    if o.dma and o.prev_target > 0:
                        if need.get(o.slot, 0) < o.prev_target:
                            need[o.slot] = o.prev_target
                    for pc in sorted(need):
                        v = need[pc]
                        if floors.get(pc, 0) >= v:
                            continue
                        eng.wait_ge(sems[pc], v)
                        floors[pc] = v
                    ins = o.emit(eng)
                    if o.inc:
                        ins.then_inc(sems[chan(o)], 16 if o.dma else 1)
                if eng_name == "sync":
                    for c in sorted(totals):
                        if c != "sync" and floors.get(c, 0) < totals[c]:
                            eng.wait_ge(sems[c], totals[c])
            return _run

        block.sync(runner("sync"))
        block.tensor(runner("tensor"))
        block.vector(runner("vector"))
        block.scalar(runner("scalar"))
        block.gpsimd(runner("gpsimd"))


def build_program():
    nc = bass.Bass("TRN2", target_bir_lowering=False)

    def din(name, shape, dt=F32):
        return nc.dram_tensor(name, list(shape), dt, kind="ExternalInput").ap()

    x_hbm = din("x", [XROWS, E])
    Wq_h = din("Wq", [E, E])
    Wkv_h = din("Wkv", [E, 2 * E])
    Wwin_h = din("Wwin", [E, 2 * NH])
    Wout_h = din("Wout", [E, E])
    qg_h = din("qg", [128, 1])
    qb_h = din("qb", [128, 1])
    kg_h = din("kg", [128, 1])
    bkrow_h = din("bkrow", [1, 128])
    bwin_h = din("bwin", [1, 2 * NH])
    Wr_h = din("Wr", [E, NEXP])
    ebias_h = din("ebias", [1, NEXP])
    Wg_h = din("Wg", [NEXP, E, FEXP])
    Wu_h = din("Wu", [NEXP, E, FEXP])
    Wd_h = din("Wd", [NEXP, FEXP, E])
    Sg_h = din("Sg", [E, E])
    Su_h = din("Su", [E, E])
    Sd_h = din("Sd", [E, E])
    ident_h = din("ident", [128, 128], BF16)
    rel_h = din("rel", [128, CPT])
    mask_h = din("edgemask", [2 * 2 * 128, 1])
    valid_h = din("validrow", [2, 1152])
    out_hbm = nc.dram_tensor("out", [TOK, E], F32, kind="ExternalOutput").ap()

    sc_dram = [nc.dram_tensor(f"scdram{j}", [128, WIDE], F32, kind="Internal").ap()
               for j in range(NT)]
    aw_dram = [nc.dram_tensor(f"awdram{j}", [128, WIDE], BF16, kind="Internal").ap()
               for j in range(NT)]

    with contextlib.ExitStack() as st:
        def sb(name, shape, dt):
            return st.enter_context(nc.sbuf_tensor(name, shape, dt))

        ident = sb("sb_ident", [128, 128], BF16)
        rel = sb("sb_rel", [128, CPT], F32)
        consts = sb("consts", [128, 8], F32)
        onesrow = sb("onesrow", [1, 128], F32)
        biasrows = sb("biasrows", [1, 64], F32)
        bkrow = sb("sb_bkrow", [1, 128], F32)
        validrow = sb("sb_validrow", [1, 1152], F32)
        gb = sb("gb", [128, 8], F32)
        wrf = sb("wrf", [128, 64], F32)
        wrhi = sb("wrhi", [128, 64], BF16)
        wrlo = sb("wrlo", [128, 64], BF16)
        bmask = sb("bmask", [128, 1152], BF16)

        R1 = sb("R1", [128, 9216], BF16)
        R2 = sb("R2", [128, 9216], BF16)
        R3 = sb("R3", [128, 9216], BF16)
        qpark = sb("qpark", [128, 8192], BF16)
        kpark = sb("kpark", [128, 9216], BF16)
        xtm = sb("xtm", [128, 8192], F32)
        wpool = sb("wpool", [128, 16640], BF16)
        swide = sb("swide", [128, WIDE], F32)
        zb = sb("zb", [128, NT * CPT], F32)
        qfm = [sb(f"qfm{j}", [128, 1024], BF16) for j in range(2)]
        bscr = [sb(f"bscr{j}", [128, 1024], BF16) for j in range(3)]
        fscr = [sb(f"fscr{j}", [128, 1024], F32) for j in range(2)]
        wp_all = sb("wp_all", [128, NT * 2 * NH], F32)
        stq = sb("stq", [128, NT * NH * 2], F32)
        stk = sb("stk", [128, NA * NH * 2], F32)
        rstq = sb("rstq", [128, NT * NH], F32)
        rstk = sb("rstk", [128, NA * NH], F32)
        mu2k = sb("mu2k", [128, NA * NH], F32)
        ccg = [sb(f"ccg{j}", [128, CPT], F32) for j in range(2)]
        abfc = [sb(f"abfc{j}", [128, CPT], BF16) for j in range(3)]
        u1st = sb("u1st", [128, 6 * NT], F32)
        lgall = sb("lgall", [128, NT * NEXP], F32)
        rwall = sb("rwall", [128, NT * NEXP], F32)
        sm8 = sb("sm8", [128, 80], F32)
        sm8b = sb("sm8b", [128, 16], F32)

        ps_A = st.enter_context(nc.psum_tensor("ps_A", [128, 1536], F32))
        ps_B = st.enter_context(nc.psum_tensor("ps_B", [128, 1536], F32))
        ps_t0 = st.enter_context(nc.psum_tensor("ps_t0", [128, 1024], BF16))
        ps_t1 = st.enter_context(nc.psum_tensor("ps_t1", [128, 1024], BF16))

        sems = {}
        for nm in ("sync", "tensor", "vector", "scalar", "gpsimd"):
            sems[nm] = st.enter_context(nc.semaphore("s_" + nm))
        for q in ("q_sync", "q_scalar", "q_gpsimd"):
            for s in range(Prog.NSLOT):
                nm = f"{q}_{s}"
                sems[nm] = st.enter_context(nc.semaphore("s_" + nm))
        block = st.enter_context(nc.Block())

        P = Prog()
        awide = swide[:].bitcast(BF16)[:, 0:WIDE]
        awideP = None

        import os as _os
        _LIM = float(_os.environ.get("KPHASE", "inf"))
        _HLIM = int(_os.environ.get("KHALF", "2"))

        class _Stop(Exception):
            pass

        def ck(x):
            if x > _LIM:
                raise _Stop()
        _S3 = float(_os.environ.get("KS3", "9"))

        def dma(eng, dst, src, tag=""):
            P.op(eng, lambda e, d=dst, s=src: e.dma_start(out=d, in_=s),
                 rd=[src], wr=[dst], dma=True, tag=tag)

        def mm(dst, lhsT, rhs, start, stop, tag=""):
            b = lhsT.base_partition()
            strips = (b // 32, (b + lhsT.partition_size() + 31) // 32)
            P.op("tensor",
                 lambda e, d=dst, l=lhsT, r=rhs, s0=start, s1=stop:
                     e.matmul(d, l, r, start=s0, stop=s1),
                 rd=[lhsT, rhs], wr=[dst], tag=tag, strips=strips)

        def tr(dst, src, tag=""):
            b = src.base_partition()
            strips = (b // 32, (b + src.partition_size() + 31) // 32)
            P.op("tensor", lambda e, d=dst, s=src: e.transpose(d, s, ident[:]),
                 rd=[src, ident[:]], wr=[dst], tag=tag, strips=strips)

        def act(out, in_, func, bias=0.0, scale=1.0, accum=None, eng="scalar",
                tag=""):
            r = [in_] + [x for x in (bias, scale) if isinstance(x, bass.AP)]
            w = [out] + ([accum] if accum is not None else [])
            P.op(eng,
                 lambda e, o=out, i=in_, f=func, b=bias, s=scale, a=accum:
                     e.activation(o, i, f, bias=b, scale=s, accum_out=a),
                 rd=r, wr=w, tag=tag)

        def scopy(out, in_, tag=""):
            P.op("scalar", lambda e, o=out, i=in_: e.copy(o, i),
                 rd=[in_], wr=[out], tag=tag)

        def vop(meth, out, ins, args=(), eng="vector", accum=None, tag=""):
            w = [out] + ([accum] if accum is not None else [])
            def emit(e, m=meth, o=out, ii=tuple(ins), aa=tuple(args), ac=accum):
                f = getattr(e, m)
                if ac is not None:
                    return f(o, *ii, *aa, accum_out=ac)
                return f(o, *ii, *aa)
            P.op(eng, emit, rd=[i for i in ins if isinstance(i, bass.AP)],
                 wr=w, tag=tag)

        def memset(apx, val, eng="vector", tag=""):
            P.op(eng, lambda e, a=apx, v=val: e.memset(a, v), wr=[apx], tag=tag)

        from concourse.hw_specs import get_activation_tables
        _tables = list(get_activation_tables(nc.m.arch).values())

        def _set_id(funcs):
            for idx, fs in enumerate(_tables):
                if all(f in fs for f in funcs):
                    return idx
            raise ValueError(f"no table set with {funcs}")

        def table(*funcs):
            sid = _set_id(funcs)

            def _emit(e, s=sid):
                e.drain()
                e.add_instruction(
                    mybir.InstLoadActFuncSet(
                        name=nc.get_next_instruction_name(),
                        act_func_set_id=s, ins=[], outs=[]))
                return e.drain()
            P.op("scalar", _emit, tag="tbl")

        def w_chunkmaj(hbm, rows, cols, row0=0):
            return bass.AP(hbm.tensor, row0 * cols,
                           [[cols, 128], [128 * cols, rows // 128], [1, cols]])

        def r3d(t, base, n, stride, inner):
            fs = t[:].ap[0][0]
            return bass.AP(t[:].tensor, base, [[fs, 128], [stride, n], [1, inner]])

        memset(onesrow[:], 1.0)
        memset(consts[:], 0.0)
        memset(consts[:, 0:1], EPS)
        dma("sync", biasrows[0:1, 0:32], bwin_h)
        dma("sync", biasrows[0:1, 32:40], ebias_h)
        dma("sync", bkrow[:], bkrow_h)
        dma("sync", ident[:], ident_h)
        dma("sync", rel[:], rel_h)
        dma("sync", wrf[:], w_chunkmaj(Wr_h, E, NEXP))
        vop("tensor_copy", wrhi[:], [wrf[:]])
        vop("tensor_tensor", wrlo[:], [wrf[:], wrhi[:]], (ALU.subtract,))
        memset(awide, 0.0)
        for j in range(NT):
            dma("sync", aw_dram[j], awide)

        WQ = wpool[:, 0:8192]
        WWIN = wpool[:, 8192:8448]
        WKVK = wpool[:, 8448:16640]
        WKVV = wpool[:, 0:8192]
        WOUT = wpool[:, 0:8192]

        gq_c, qbS_c, kg_c = gb[:, 0:1], gb[:, 6:7], gb[:, 2:3]
        mlo_c, mhi_c = gb[:, 4:5], gb[:, 5:6]

        for half in range(_HLIM):
          try:
            B = HT * half
            hx = f"h{half}"

            dma("sync", gb[:, 0:1], qg_h)
            dma("sync", gb[:, 1:2], qb_h)
            dma("sync", gb[:, 2:3], kg_h)
            dma("sync", gb[:, 4:5],
                bass.AP(mask_h.tensor, (half * 2 + 0) * 128, [[1, 128], [1, 1]]))
            dma("sync", gb[:, 5:6],
                bass.AP(mask_h.tensor, (half * 2 + 1) * 128, [[1, 128], [1, 1]]))
            dma("sync", validrow[:],
                bass.AP(valid_h.tensor, half * 1152, [[1152, 1], [1, 1152]]))
            vop("tensor_scalar", gb[:, 6:7], [gb[:, 1:2], SCALE, None], (ALU.mult,))

            dma("scalar", xtm[:].rearrange("p (i n) -> p i n", i=NT),
                bass.AP(x_hbm.tensor, (8 + B) * E,
                        [[E, 128], [128 * E, NT], [1, E]]), tag=f"xin{hx}")

            dma("gpsimd", WQ.rearrange("p (c n) -> p c n", c=8),
                w_chunkmaj(Wq_h, E, E), tag=f"Wq{hx}")
            dma("gpsimd", WWIN.rearrange("p (c n) -> p c n", c=8),
                w_chunkmaj(Wwin_h, E, 2 * NH), tag=f"Wwin{hx}")
            dma("gpsimd", WKVK.rearrange("p (c n) -> p c n", c=8),
                bass.AP(Wkv_h.tensor, 0,
                        [[2 * E, 128], [128 * 2 * E, 8], [1, E]]),
                tag=f"Wkvk{hx}")

            for j in range(3):
                w = 512 if j < 2 else 128
                pg = ps_A[:, 0:512]
                mm(pg[:, 0:w], bkrow[:], validrow[0:1, 512 * j:512 * j + w],
                   True, True, tag=f"bmask{hx}")
                vop("tensor_copy", bmask[:, 512 * j:512 * j + w], [pg[:, 0:w]],
                    tag=f"bmaske{hx}")

            def emit_s0(Bv, hx2):
                for a in range(NA):
                    xb = bscr[a % 2]
                    dma("gpsimd", xb[:],
                        bass.AP(x_hbm.tensor, (Bv + 128 * a) * E,
                                [[E, 128], [1, E]]),
                        tag=f"xbf{hx2}")
                    pst = (ps_t0, ps_t1)[a % 2]
                    for c in range(8):
                        tr(pst[:, 128 * c:128 * c + 128],
                           xb[:, 128 * c:128 * c + 128], tag=f"xT{hx2}")
                    scopy(r3d(R1, 128 * a, 8, 1152, 128), pst[:],
                          tag=f"xfm{hx2}")

            ck(0)
            if half == 0:
                emit_s0(B, hx)

            ck(1)
            table(AF.Silu)
            g = [ps_A[:, 0:512], ps_A[:, 512:1024], ps_A[:, 1024:1536],
                 ps_B[:, 0:512], ps_B[:, 512:1024], ps_B[:, 1024:1536]]

            def xfm(c, col, width=128):
                return R1[:, 1152 * c + col: 1152 * c + col + width]

            for i in range(NT):
                qp = fscr[i % 2]
                for gi in range(2):
                    pg = g[(2 * i + gi) % 4]
                    for c in range(8):
                        mm(pg, xfm(c, 8 + 128 * i),
                           WQ[:, 1024 * c + 512 * gi:1024 * c + 512 * gi + 512],
                           c == 0, c == 7, tag=f"qmm{hx}")
                    act(qp[:, 512 * gi:512 * gi + 512], pg, AF.Silu, tag=f"qev{hx}")
                vop("tensor_copy", qpark[:, 1024 * i:1024 * (i + 1)], [qp[:]],
                    eng="gpsimd", tag=f"qpark{hx}")
                vop("reduce_sum", stq[:, 32 * i:32 * i + 16],
                    [qp[:].rearrange("p (h d) -> p h d", h=NH)], (AX.X,),
                    tag=f"qsum{hx}")
                sq = qfm[0]
                act(sq[:], qp[:], AF.Square, tag=f"qsq{hx}")
                vop("reduce_sum", stq[:, 32 * i + 16:32 * i + 32],
                    [sq[:].rearrange("p (h d) -> p h d", h=NH)], (AX.X,),
                    tag=f"qsqs{hx}")
                pw = g[4 + (i % 2)]
                for c in range(8):
                    mm(pw[:, 0:32], xfm(c, 8 + 128 * i),
                       WWIN[:, 32 * c:32 * c + 32], c == 0, False, tag=f"wpmm{hx}")
                mm(pw[:, 0:32], onesrow[:], biasrows[0:1, 0:32], False, True,
                   tag=f"wpb{hx}")
                act(wp_all[:, 32 * i:32 * i + 32], pw[:, 0:32], AF.Silu,
                    tag=f"wpev{hx}")

            dma("gpsimd", WKVV.rearrange("p (c n) -> p c n", c=8),
                bass.AP(Wkv_h.tensor, E, [[2 * E, 128], [128 * 2 * E, 8], [1, E]]),
                tag=f"Wkvv{hx}")
            for a in range(NA):
                kp = fscr[a % 2]
                for gi in range(2):
                    pg = g[(2 * a + gi) % 4]
                    for c in range(8):
                        mm(pg, xfm(c, 128 * a),
                           WKVK[:, 1024 * c + 512 * gi:1024 * c + 512 * gi + 512],
                           c == 0, c == 7, tag=f"kmm{hx}")
                    act(kp[:, 512 * gi:512 * gi + 512], pg, AF.Silu, tag=f"kev{hx}")
                vop("tensor_copy", kpark[:, 1024 * a:1024 * (a + 1)], [kp[:]],
                    eng="gpsimd", tag=f"kpark{hx}")
                vop("reduce_sum", stk[:, 32 * a:32 * a + 16],
                    [kp[:].rearrange("p (h d) -> p h d", h=NH)], (AX.X,),
                    tag=f"ksum{hx}")
                sq = qfm[0]
                act(sq[:], kp[:], AF.Square, tag=f"ksq{hx}")
                vop("reduce_sum", stk[:, 32 * a + 16:32 * a + 32],
                    [sq[:].rearrange("p (h d) -> p h d", h=NH)], (AX.X,),
                    tag=f"ksqs{hx}")
            for a in range(NA):
                for gi in range(2):
                    pg = g[(2 * a + gi) % 4]
                    for c in range(8):
                        mm(pg, xfm(c, 128 * a),
                           WKVV[:, 1024 * c + 512 * gi:1024 * c + 512 * gi + 512],
                           c == 0, c == 7, tag=f"vmm{hx}")
                    act(R3[:, 1024 * a + 512 * gi:1024 * a + 512 * gi + 512],
                        pg, AF.Silu, tag=f"vev{hx}")

            ck(2)
            table(AF.Ln, AF.Exp)
            for (stx, rst, mu2, n, scl) in ((stq, rstq, mu2k, NT, SCALE),
                                            (stk, rstk, mu2k, NA, 1.0)):
                su = r3d(stx, 0, n, 32, 16)
                sq_ = r3d(stx, 16, n, 32, 16)
                vop("tensor_scalar", su, [su, 1.0 / D, None], (ALU.mult,))
                vop("tensor_scalar", sq_, [sq_, 1.0 / D, None], (ALU.mult,))
                m3 = mu2[:, 0:n * NH].rearrange("p (n h) -> p n h", n=n)
                vop("tensor_tensor", m3, [su, su], (ALU.mult,))
                r3 = rst[:].rearrange("p (n h) -> p n h", n=n)
                vop("tensor_tensor", r3, [sq_, m3], (ALU.subtract,))
                act(rst[:], rst[:], AF.Ln, bias=consts[:, 0:1])
                act(rst[:], rst[:], AF.Exp, scale=-0.5)
                if scl != 1.0:
                    vop("tensor_scalar", rst[:], [rst[:], scl, None], (ALU.mult,))

            wrec = zb[:, 0:NT * 2 * NH]
            act(wrec, wp_all[:], AF.Exp, scale=-1.0, tag="wsexp")
            vop("tensor_scalar", wrec, [wrec, 1.0, None], (ALU.add,))
            vop("reciprocal", wp_all[:], [wrec])
            wv = r3d(wp_all, 0, NT, 32, 16)
            sv = r3d(wp_all, 16, NT, 32, 16)
            vop("tensor_scalar", wv, [wv, 8.0, 0.5], (ALU.mult, ALU.add))
            vop("tensor_scalar", sv, [sv, 9.5, 0.5], (ALU.mult, ALU.add))

            ck(3)
            for i in range(NT):
                z3 = r3d(zb, CPT * i, NH, K17, K17)
                wbc = wp_all[:, 32 * i:32 * i + 16].unsqueeze(2).broadcast_to(
                    [128, NH, K17])
                sbc = wp_all[:, 32 * i + 16:32 * i + 32].unsqueeze(2).broadcast_to(
                    [128, NH, K17])
                vop("tensor_tensor", z3,
                    [rel[:].rearrange("p (h w) -> p h w", h=NH), wbc],
                    (ALU.subtract,), tag=f"z1{hx}")
                vop("tensor_tensor", z3, [z3, sbc], (ALU.mult,), tag=f"z2{hx}")
            table(AF.Sigmoid)
            act(zb[:], zb[:], AF.Sigmoid, tag=f"B{hx}")
            table(AF.Ln, AF.Exp)
            dma("gpsimd", WOUT.rearrange("p (c n) -> p c n", c=8),
                w_chunkmaj(Wout_h, E, E), tag=f"Wout{hx}")
            ATa = [R1[:, 0:1024], R1[:, 1024:2048]]
            ATb = [R1[0:16, 2048:3072], R1[0:16, 3072:4096]]

            def emit_A(i):
                dma("sync", ccg[i % 2][:].rearrange("p (h w) -> p h w", h=NH),
                    bass.AP(sc_dram[i].tensor, 0,
                            [[2305, 128], [144, NH], [1, K17]]), tag=f"scg{hx}")
                lg = ccg[i % 2]
                vop("scalar_tensor_tensor", lg[:],
                    [zb[:, CPT * i:CPT * (i + 1)], -PEN, lg[:]],
                    (ALU.mult, ALU.add), tag=f"lg{hx}")
                lg3 = lg[:].rearrange("p (h w) -> p h w", h=NH)
                P.op("vector",
                     lambda e, o=sm8[:, 0:16], ii=lg3:
                         e.reduce_max(o, ii, axis=AX.X, negate=True),
                     rd=[lg3], wr=[sm8[:, 0:16]], tag=f"mxn{hx}")
                nmb = sm8[:, 0:16].unsqueeze(2).broadcast_to([128, NH, K17])
                vop("tensor_tensor", lg3, [lg3, nmb], (ALU.add,), tag=f"sub{hx}")
                act(lg[:], lg[:], AF.Exp, tag=f"exp{hx}")
                vop("reduce_sum", sm8[:, 16:32], [lg3], (AX.X,), tag=f"sum{hx}")
                vop("reciprocal", sm8[:, 32:48], [sm8[:, 16:32]], tag=f"rin{hx}")
                rib = sm8[:, 32:48].unsqueeze(2).broadcast_to([128, NH, K17])
                vop("tensor_tensor",
                    abfc[i % 3][:].rearrange("p (h w) -> p h w", h=NH),
                    [lg3, rib], (ALU.mult,), tag=f"anrm{hx}")
                dma("sync",
                    bass.AP(aw_dram[i].tensor, 0,
                            [[2305, 128], [144, NH], [1, K17]]),
                    abfc[i % 3][:].rearrange("p (h w) -> p h w", h=NH),
                    tag=f"asc{hx}")

            awideP = [R1[:, 6144:6144 + WIDE], kpark[:, 0:WIDE]]

            def emit_B(i):
                aw = awideP[i % 2]
                dma("sync", aw, aw_dram[i], tag=f"awrb{hx}")
                ao_tm = R1[:, 4096 + 1024 * (i % 2):4096 + 1024 * (i % 2) + 1024]
                for w in range(2):
                    for hh in range(8):
                        s = 8 * w + hh
                        tr(ps_t0[:, 128 * hh:128 * hh + 128],
                           aw[:, 144 * s:144 * s + 128], tag=f"aT{hx}")
                        tr(ps_t1[0:16, 128 * hh:128 * hh + 128],
                           aw[:, 144 * s + 128:144 * s + 144], tag=f"aTb{hx}")
                    scopy(ATa[w], ps_t0[:], tag=f"aTe{hx}")
                    scopy(ATb[w], ps_t1[0:16, :], tag=f"aTbe{hx}")
                    av = (ps_A[:, 0:512], ps_A[:, 512:1024])[w]
                    for hh in range(8):
                        s = 8 * w + hh
                        mm(av[:, 64 * hh:64 * hh + 64],
                           ATa[w][:, 128 * hh:128 * hh + 128],
                           R3[:, 1024 * i + 64 * s:1024 * i + 64 * s + 64],
                           True, False, tag=f"av{hx}")
                        mm(av[:, 64 * hh:64 * hh + 64],
                           ATb[w][0:16, 128 * hh:128 * hh + 128],
                           R3[0:16, 1024 * (i + 1) + 64 * s:
                              1024 * (i + 1) + 64 * s + 64],
                           False, True, tag=f"avb{hx}")
                    scopy(ao_tm[:, 512 * w:512 * w + 512], av, tag=f"aoev{hx}")
                for c in range(8):
                    tr(ps_t0[:, 128 * c:128 * c + 128],
                       ao_tm[:, 128 * c:128 * c + 128], tag=f"aoT{hx}")
                scopy(qpark[:, 1024 * i:1024 * (i + 1)], ps_t0[:],
                      tag=f"aofm{hx}")

            def knorm(a):
                kt = bscr[2]
                kt3 = kt[:].rearrange("p (h d) -> p h d", h=NH)
                kp3 = bass.AP(kpark[:].tensor, 1024 * a,
                              [[9216, 128], [64, NH], [1, D]])
                mub = stk[:, 32 * a:32 * a + 16].unsqueeze(2).broadcast_to(
                    [128, NH, D])
                rsb = rstk[:, 16 * a:16 * a + 16].unsqueeze(2).broadcast_to(
                    [128, NH, D])
                vop("tensor_tensor", kt3, [kp3, mub], (ALU.subtract,),
                    tag=f"knrm{hx}")
                vop("tensor_tensor", kt3, [kt3, rsb], (ALU.mult,),
                    tag=f"knrm2{hx}")
                if a == 0:
                    vop("tensor_scalar", kt[:], [kt[:], mlo_c, None], (ALU.mult,),
                        tag=f"kmask{hx}")
                if a == NA - 1:
                    vop("tensor_scalar", kt[:], [kt[:], mhi_c, None], (ALU.mult,),
                        tag=f"kmask{hx}")
                pst = (ps_t0, ps_t1)[a % 2]
                for c in range(8):
                    tr(pst[:, 128 * c:128 * c + 128], kt[:, 128 * c:128 * c + 128],
                       tag=f"kT{hx}")
                act(r3d(R2, 128 * a, 8, 1152, 128), pst[:], AF.Copy,
                    scale=kg_c, tag=f"kfm{hx}")

            def qnorm(i):
                qt = bscr[i % 2]
                qt3 = qt[:].rearrange("p (h d) -> p h d", h=NH)
                qp3 = bass.AP(qpark[:].tensor, 1024 * i,
                              [[8192, 128], [64, NH], [1, D]])
                mub = stq[:, 32 * i:32 * i + 16].unsqueeze(2).broadcast_to(
                    [128, NH, D])
                rsb = rstq[:, 16 * i:16 * i + 16].unsqueeze(2).broadcast_to(
                    [128, NH, D])
                vop("tensor_tensor", qt3, [qp3, mub], (ALU.subtract,),
                    tag=f"qnrm{hx}")
                vop("tensor_tensor", qt3, [qt3, rsb], (ALU.mult,),
                    tag=f"qnrm2{hx}")

            knorm(0)
            knorm(1)
            qnorm(0)
            for i in range(NT):
                if i + 2 <= NT:
                    knorm(i + 2)
                if i + 1 < NT:
                    qnorm(i + 1)
                qt = bscr[i % 2]
                pst = (ps_t0, ps_t1)[i % 2]
                for c in range(8):
                    tr(pst[:, 128 * c:128 * c + 128], qt[:, 128 * c:128 * c + 128],
                       tag=f"qT{hx}")
                vop("tensor_scalar", qfm[i % 2][:], [pst[:], gq_c, qbS_c],
                    (ALU.mult, ALU.add), tag=f"qfme{hx}")

                if _S3 < 2:
                    continue
                for s in [x for x in range(NH) if x % 2 == 0] + \
                         [x for x in range(NH) if x % 2 == 1]:
                    if s < 9:
                        dst = ps_A[:, 512 * (s // 3) + 144 * (s % 3):
                                   512 * (s // 3) + 144 * (s % 3) + 144]
                    else:
                        j = s - 9
                        dst = ps_B[:, 512 * (j // 3) + 144 * (j % 3):
                                   512 * (j // 3) + 144 * (j % 3) + 144]
                    lhs = qfm[i % 2][64 * (s % 2):64 * (s % 2) + 64,
                                     128 * (s // 2):128 * (s // 2) + 128]
                    rhs = R2[64 * (s % 2):64 * (s % 2) + 64,
                             1152 * (s // 2) + 128 * i:1152 * (s // 2) + 128 * i + 144]
                    mm(dst, lhs, rhs, True, False, tag=f"sc{hx}")
                    rhsb = bmask[64 * (s % 2):64 * (s % 2) + 64,
                                 128 * i:128 * i + 144]
                    mm(dst, lhs, rhsb, False, True, tag=f"scb{hx}")
                if _S3 < 3:
                    continue
                fs_sw = swide[:].ap[0][0]
                srcA = bass.AP(ps_A[:].tensor, int(ps_A[:].offset),
                               [[1536, 128], [512, 3], [144, 3], [1, 144]])
                dstA = bass.AP(swide[:].tensor, 0, [[fs_sw, 128], [432, 3],
                                                    [144, 3], [1, 144]])
                scopy(dstA, srcA, tag=f"scevA{hx}")
                dma("sync", bass.AP(sc_dram[i].tensor, 0,
                                    [[2304, 128], [1, 1296]]),
                    swide[:, 0:1296], tag=f"scoutA{hx}")
                srcB1 = bass.AP(ps_B[:].tensor, int(ps_B[:].offset),
                                [[1536, 128], [512, 2], [144, 3], [1, 144]])
                dstB1 = bass.AP(swide[:].tensor, 1296, [[fs_sw, 128], [432, 2],
                                                        [144, 3], [1, 144]])
                scopy(dstB1, srcB1, tag=f"scevB{hx}")
                srcB2 = bass.AP(ps_B[:].tensor, int(ps_B[:].offset) + 1024,
                                [[1536, 128], [1, 144]])
                scopy(swide[:, 2160:2304], srcB2, tag=f"scevB2{hx}")
                dma("sync", bass.AP(sc_dram[i].tensor, 1296,
                                     [[2304, 128], [1, 1008]]),
                    swide[:, 1296:2304], tag=f"scoutB{hx}")
                if i >= 2:
                    emit_B(i - 2)
                emit_A(i)
            emit_B(NT - 2)
            emit_B(NT - 1)

            ck(6)
            table(AF.Silu)
            def s6_tail(i):
                ws_o = fscr[i % 2]
                for gi in range(2):
                    pg = g[(2 * i + gi) % 4]
                    act(ws_o[:, 512 * gi:512 * gi + 512], pg, AF.Silu,
                        tag=f"woev{hx}")
                xsl = xtm[:, 1024 * i:1024 * (i + 1)]
                vop("scalar_tensor_tensor", xsl, [ws_o[:], 1.0, xsl],
                    (ALU.bypass, ALU.add), accum=u1st[:, i:i + 1], tag=f"u1{hx}")
                act(qfm[0][:], xsl, AF.Square, accum=u1st[:, NT + i:NT + i + 1],
                    tag=f"u1sq{hx}")

            for i in range(NT):
                ao_fm = qpark[:, 1024 * i:1024 * (i + 1)]
                for gi in range(2):
                    pg = g[(2 * i + gi) % 4]
                    for c in range(8):
                        mm(pg, ao_fm[:, 128 * c:128 * c + 128],
                           WOUT[:, 1024 * c + 512 * gi:1024 * c + 512 * gi + 512],
                           c == 0, c == 7, tag=f"wo{hx}")
                if i >= 1:
                    s6_tail(i - 1)
            s6_tail(NT - 1)

            ck(7)
            table(AF.Ln, AF.Exp)
            mu_u = u1st[:, 0:NT]
            sq_u = u1st[:, NT:2 * NT]
            m2_u = u1st[:, 2 * NT:3 * NT]
            rs_u = u1st[:, 3 * NT:4 * NT]
            vop("tensor_scalar", mu_u, [mu_u, 1.0 / E, None], (ALU.mult,))
            vop("tensor_scalar", sq_u, [sq_u, 1.0 / E, None], (ALU.mult,))
            vop("tensor_tensor", m2_u, [mu_u, mu_u], (ALU.mult,))
            vop("tensor_tensor", rs_u, [sq_u, m2_u], (ALU.subtract,))
            act(rs_u, rs_u, AF.Ln, bias=consts[:, 0:1])
            act(rs_u, rs_u, AF.Exp, scale=-0.5)
            for i in range(NT):
                xsl = xtm[:, 1024 * i:1024 * (i + 1)]
                vop("tensor_scalar", xsl,
                    [xsl, mu_u[:, i:i + 1], rs_u[:, i:i + 1]],
                    (ALU.subtract, ALU.mult), tag=f"x1{hx}")
            for i in range(NT):
                xsl = xtm[:, 1024 * i:1024 * (i + 1)]
                x1b = bscr[i % 2]
                vop("tensor_copy", x1b[:], [xsl], eng="gpsimd", tag=f"x1b{hx}")
                pst = (ps_t0, ps_t1)[i % 2]
                for c in range(8):
                    tr(pst[:, 128 * c:128 * c + 128], x1b[:, 128 * c:128 * c + 128],
                       tag=f"x1T{hx}")
                scopy(r3d(R3, 128 * i, 8, 1024, 128), pst[:], tag=f"x1fm{hx}")
                lob = bscr[2]
                vop("tensor_tensor", lob[:], [xsl, x1b[:]], (ALU.subtract,),
                    tag=f"lo{hx}")
                for c in range(8):
                    tr(pst[:, 128 * c:128 * c + 128], lob[:, 128 * c:128 * c + 128],
                       tag=f"loT{hx}")
                scopy(qfm[i % 2][:], pst[:], tag=f"lofm{hx}")
                pg = g[i % 2]
                for c in range(8):
                    mm(pg[:, 0:8], R3[:, 1024 * c + 128 * i:1024 * c + 128 * i + 128],
                       wrhi[:, 8 * c:8 * c + 8], c == 0, False, tag=f"rt1{hx}")
                for c in range(8):
                    mm(pg[:, 0:8], R3[:, 1024 * c + 128 * i:1024 * c + 128 * i + 128],
                       wrlo[:, 8 * c:8 * c + 8], False, False, tag=f"rt2{hx}")
                for c in range(8):
                    mm(pg[:, 0:8], qfm[i % 2][:, 128 * c:128 * c + 128],
                       wrhi[:, 8 * c:8 * c + 8], False, False, tag=f"rt3{hx}")
                mm(pg[:, 0:8], onesrow[:], biasrows[0:1, 32:40], False, True,
                   tag=f"rtb{hx}")
                vop("tensor_copy", lgall[:, 8 * i:8 * i + 8], [pg[:, 0:8]],
                    tag=f"lgev{hx}")

            ck(8)
            table(AF.Silu)
            lg3a = lgall[:].rearrange("p (i e) -> p i e", i=NT)
            vop("reduce_max", sm8[:, 0:8], [lg3a], (AX.X,), tag=f"rm1{hx}")
            m1b = sm8[:, 0:8].unsqueeze(2).broadcast_to([128, NT, NEXP])
            rw3 = rwall[:].rearrange("p (i e) -> p i e", i=NT)
            vop("tensor_tensor", rw3, [lg3a, m1b], (ALU.is_equal,), tag=f"oh1{hx}")
            msk3 = sm8[:, 16:80].rearrange("p (i e) -> p i e", i=NT)
            vop("scalar_tensor_tensor", msk3, [rw3, -1e30, lg3a],
                (ALU.mult, ALU.add), tag=f"mskd{hx}")
            vop("reduce_max", sm8[:, 8:16], [msk3], (AX.X,), tag=f"rm2{hx}")
            m2b = sm8[:, 8:16].unsqueeze(2).broadcast_to([128, NT, NEXP])
            vop("tensor_tensor", lg3a, [msk3, m2b], (ALU.is_equal,), tag=f"oh2{hx}")
            vop("tensor_tensor", sm8[:, 0:8], [sm8[:, 8:16], sm8[:, 0:8]],
                (ALU.subtract,), tag=f"d{hx}")
            vop("tensor_scalar", sm8[:, 0:8], [sm8[:, 0:8], -1e-20, None],
                (ALU.min,), tag=f"dg{hx}")
            act(sm8[:, 8:16], sm8[:, 0:8], AF.Silu, tag=f"sild{hx}")
            vop("reciprocal", sm8b[:, 0:8], [sm8[:, 0:8]], tag=f"drec{hx}")
            vop("tensor_tensor", sm8b[:, 8:16], [sm8[:, 8:16], sm8b[:, 0:8]],
                (ALU.mult,), tag=f"p2{hx}")
            vop("tensor_tensor", msk3, [lg3a, rw3], (ALU.subtract,), tag=f"dd{hx}")
            p2b = sm8b[:, 8:16].unsqueeze(2).broadcast_to([128, NT, NEXP])
            vop("tensor_tensor", msk3, [msk3, p2b], (ALU.mult,), tag=f"ddp{hx}")
            vop("tensor_tensor", rw3, [msk3, rw3], (ALU.add,), tag=f"rw{hx}")

            groups = [("sh",), (0, 1), (2, 3), (4, 5), (6, 7)]
            WEXTA = qpark[:, 0:8192]
            WEXTB = kpark[:, 0:8192]

            def gw_regions(gi_):
                if gi_ % 2 == 0:
                    return wpool[:, 0:8192], wpool[:, 8192:16384], wpool[:, 0:8192]
                return WEXTA, WEXTB, WEXTA

            def stream_p1(gi_):
                if gi_ >= len(groups):
                    return
                grp = groups[gi_]
                RA, RB, _ = gw_regions(gi_)
                if grp[0] == "sh":
                    dma("gpsimd", RA.rearrange("p (c n) -> p c n", c=8),
                        w_chunkmaj(Sg_h, E, E), tag=f"Sg{hx}")
                    dma("gpsimd", RB.rearrange("p (c n) -> p c n", c=8),
                        w_chunkmaj(Su_h, E, E), tag=f"Su{hx}")
                else:
                    for k_, e_ in enumerate(grp):
                        dma("gpsimd",
                            RA[:, 4096 * k_:4096 * (k_ + 1)].rearrange(
                                "p (c n) -> p c n", c=8),
                            bass.AP(Wg_h.tensor, e_ * E * FEXP,
                                    [[FEXP, 128], [128 * FEXP, 8], [1, FEXP]]),
                            tag=f"Wg{e_}{hx}")
                        dma("gpsimd",
                            RB[:, 4096 * k_:4096 * (k_ + 1)].rearrange(
                                "p (c n) -> p c n", c=8),
                            bass.AP(Wu_h.tensor, e_ * E * FEXP,
                                    [[FEXP, 128], [128 * FEXP, 8], [1, FEXP]]),
                            tag=f"Wu{e_}{hx}")

            stream_p1(0)
            for gi_, grp in enumerate(groups):
                RA, RB, RD_ = gw_regions(gi_)

                ng = len(grp)
                Fh = E if grp[0] == "sh" else FEXP
                pairs = [(i, k_, e_) for i in range(NT)
                         for k_, e_ in enumerate(grp)]
                gslots = [g[0], g[1]]
                uslots = [ps_A[:, 1024:1536], ps_B[:, 1024:1536]]
                pcnt = [0]
                rec = {}

                def pair_mms(j):
                    i, k_, e_ = pairs[j]
                    slots = []
                    for fg in range(Fh // 512):
                        pg = gslots[pcnt[0] % 2]
                        pu = uslots[pcnt[0] % 2]
                        pcnt[0] += 1
                        slots.append((pg, pu))
                        for c in range(8):
                            WGsl = RA[:, 1024 * c + 512 * fg:
                                      1024 * c + 512 * fg + 512] \
                                if grp[0] == "sh" else \
                                RA[:, 4096 * k_ + 512 * c:4096 * k_ + 512 * c + 512]
                            mm(pg, R3[:, 1024 * c + 128 * i:
                                      1024 * c + 128 * i + 128],
                               WGsl, c == 0, c == 7, tag=f"gmm{hx}")
                        for c in range(8):
                            WUsl = RB[:, 1024 * c + 512 * fg:
                                      1024 * c + 512 * fg + 512] \
                                if grp[0] == "sh" else \
                                RB[:, 4096 * k_ + 512 * c:4096 * k_ + 512 * c + 512]
                            mm(pu, R3[:, 1024 * c + 128 * i:
                                      1024 * c + 128 * i + 128],
                               WUsl, c == 0, c == 7, tag=f"umm{hx}")
                    rec[j] = slots

                def pair_tail(j):
                    i, k_, e_ = pairs[j]
                    gbuf = fscr[j % 2]
                    hbuf = bscr[j % 2]
                    for fg, (pg, pu) in enumerate(rec.pop(j)):
                        act(gbuf[:, 512 * fg:512 * fg + 512], pg, AF.Silu,
                            tag=f"gev{hx}")
                        if grp[0] == "sh":
                            vop("tensor_tensor", hbuf[:, 512 * fg:512 * fg + 512],
                                [gbuf[:, 512 * fg:512 * fg + 512], pu],
                                (ALU.mult,), tag=f"hsh{hx}")
                        else:
                            vop("scalar_tensor_tensor",
                                hbuf[:, 512 * fg:512 * fg + 512],
                                [pu, rwall[:, 8 * i + e_:8 * i + e_ + 1],
                                 gbuf[:, 512 * fg:512 * fg + 512]],
                                (ALU.mult, ALU.mult), tag=f"hrw{hx}")
                    nch = Fh // 128
                    pst = (ps_t0, ps_t1)[j % 2]
                    for c in range(nch):
                        tr(pst[:, 128 * c:128 * c + 128],
                           hbuf[:, 128 * c:128 * c + 128], tag=f"hT{hx}")
                    base = (i * ng + k_) * (512 if grp[0] != "sh" else 1024)
                    scopy(R2[:, base:base + 128 * nch],
                          pst[:, 0:128 * nch], tag=f"hpk{hx}")

                skew = 0 if grp[0] == "sh" else 1
                for j in range(len(pairs)):
                    pair_mms(j)
                    if j >= skew:
                        pair_tail(j - skew)
                for j in range(len(pairs) - skew, len(pairs)):
                    pair_tail(j)

                stream_p1(gi_ + 1)
                if grp[0] == "sh":
                    dma("gpsimd", RD_.rearrange("p (c n) -> p c n", c=8),
                        w_chunkmaj(Sd_h, E, E), tag=f"Sd{hx}")
                else:
                    for k_, e_ in enumerate(grp):
                        dma("gpsimd",
                            RD_[:, 4096 * k_:4096 * (k_ + 1)].rearrange(
                                "p (c n) -> p c n", c=4),
                            bass.AP(Wd_h.tensor, e_ * FEXP * E,
                                    [[E, 128], [128 * E, 4], [1, E]]),
                            tag=f"Wd{e_}{hx}")

                last_grp = gi_ == len(groups) - 1

                def eo_mms(i):
                    eo = (ps_B[:, 0:1024], ps_A[:, 0:1024])[i % 2]
                    first = True
                    for k_, e_ in enumerate(grp):
                        nch = (E if grp[0] == "sh" else FEXP) // 128
                        base = (i * ng + k_) * (512 if grp[0] != "sh" else 1024)
                        for c in range(nch):
                            for gi2 in range(2):
                                WDsl = RD_[:, 1024 * c + 512 * gi2:
                                           1024 * c + 512 * gi2 + 512] \
                                    if grp[0] == "sh" else \
                                    RD_[:, 4096 * k_ + 1024 * c + 512 * gi2:
                                        4096 * k_ + 1024 * c + 512 * gi2 + 512]
                                mm(eo[:, 512 * gi2:512 * gi2 + 512],
                                   R2[:, base + 128 * c:base + 128 * c + 128],
                                   WDsl, first and c == 0,
                                   (k_ == ng - 1) and c == nch - 1,
                                   tag=f"eo{hx}")
                        first = False

                def eo_tail(i):
                    eo = (ps_B[:, 0:1024], ps_A[:, 0:1024])[i % 2]
                    for gi2 in range(2):
                        xsl = xtm[:, 1024 * i + 512 * gi2:1024 * i + 512 * gi2 + 512]
                        acc_ap = (u1st[:, 4 * NT + i:4 * NT + i + 1] if
                                  (last_grp and gi2 == 0) else
                                  (u1st[:, 5 * NT + i:5 * NT + i + 1] if
                                   (last_grp and gi2 == 1) else None))
                        vop("scalar_tensor_tensor", xsl,
                            [eo[:, 512 * gi2:512 * gi2 + 512], 1.0, xsl],
                            (ALU.bypass, ALU.add), accum=acc_ap, tag=f"eoadd{hx}")
                    if last_grp:
                        act(qfm[0][:], xtm[:, 1024 * i:1024 * (i + 1)], AF.Square,
                            accum=u1st[:, NT + i:NT + i + 1], tag=f"u2sq{hx}")

                for i in range(NT):
                    eo_mms(i)
                    if i >= 1:
                        eo_tail(i - 1)
                eo_tail(NT - 1)

            ck(9)
            table(AF.Ln, AF.Exp)
            vop("tensor_tensor", mu_u, [u1st[:, 4 * NT:5 * NT],
                                        u1st[:, 5 * NT:6 * NT]], (ALU.add,))
            vop("tensor_scalar", mu_u, [mu_u, 1.0 / E, None], (ALU.mult,))
            vop("tensor_scalar", sq_u, [u1st[:, NT:2 * NT], 1.0 / E, None],
                (ALU.mult,))
            vop("tensor_tensor", m2_u, [mu_u, mu_u], (ALU.mult,))
            vop("tensor_tensor", rs_u, [sq_u, m2_u], (ALU.subtract,))
            act(rs_u, rs_u, AF.Ln, bias=consts[:, 0:1])
            act(rs_u, rs_u, AF.Exp, scale=-0.5)
            for i in range(NT):
                xsl = xtm[:, 1024 * i:1024 * (i + 1)]
                vop("tensor_scalar", xsl,
                    [xsl, mu_u[:, i:i + 1], rs_u[:, i:i + 1]],
                    (ALU.subtract, ALU.mult), tag=f"x2{hx}")
                dma("scalar",
                    bass.AP(out_hbm.tensor, (HT * half + 128 * i) * E,
                            [[E, 128], [1, E]]),
                    xsl, tag=f"out{hx}")
            if half == 0 and _HLIM > 1:
                emit_s0(HT, "h1")
          except _Stop:
            pass

        P.build(block, sems)
    return nc


_NC_CACHE = []


def kernel(**inputs):
    x = np.ascontiguousarray(np.asarray(inputs["L0_attn_x"], dtype=np.float32))
    Bt, L, _ = x.shape
    per_core = (Bt * L) // NCORES
    cores_per_batch = L // per_core

    def f32(v):
        return np.ascontiguousarray(np.asarray(v, dtype=np.float32))

    Wq = f32(inputs["L0_attn_Wq"])
    Wkv = f32(inputs["L0_attn_Wkv"])
    Wwin = f32(inputs["L0_attn_Wwin"])
    Wout = f32(inputs["L0_attn_Wout"])
    qg = np.tile(f32(inputs["L0_attn_q_g"]), 2)[:, None].copy()
    qb = np.tile(f32(inputs["L0_attn_q_b"]), 2)[:, None].copy()
    kg = np.tile(f32(inputs["L0_attn_k_g"]), 2)[:, None].copy()
    bkrow = np.tile(f32(inputs["L0_attn_k_b"]), 2)[None, :].copy()
    bwin = f32(inputs["L0_attn_bwin"])[None, :].copy()
    Wr = f32(inputs["L0_ffn_Wr"])
    ebias = f32(inputs["L0_ffn_expert_bias"])[None, :].copy()
    Wg = f32(inputs["L0_ffn_Wg"])
    Wu = f32(inputs["L0_ffn_Wu"])
    Wd = f32(inputs["L0_ffn_Wd"])
    Sg = f32(inputs["L0_ffn_Sg"])
    Su = f32(inputs["L0_ffn_Su"])
    Sd = f32(inputs["L0_ffn_Sd"])

    ident = np.eye(128, dtype=bfd)
    relv = np.abs(np.arange(K17, dtype=np.float32) - (K17 // 2))
    rel = np.broadcast_to(np.tile(relv, NH)[None, :], (128, CPT)).copy()

    in_maps = []
    for core in range(NCORES):
        b = core // cores_per_batch
        off = (core % cores_per_batch) * per_core
        xh = np.zeros((XROWS, E), dtype=np.float32)
        lo = off - 8
        hi = off + per_core + 8
        s_lo = max(lo, 0)
        s_hi = min(hi, L)
        xh[s_lo - lo:s_hi - lo, :] = x[b, s_lo:s_hi, :]
        em = np.zeros((2, 2, 128, 1), dtype=np.float32)
        vr = np.zeros((2, 1152), dtype=np.float32)
        for half in range(2):
            Bh = HT * half
            for p in range(128):
                em[half, 0, p, 0] = 1.0 if 0 <= (lo + Bh + p) < L else 0.0
                em[half, 1, p, 0] = 1.0 if 0 <= (lo + Bh + 1024 + p) < L else 0.0
            rr = np.arange(1152)
            vr[half] = ((lo + Bh + rr) >= 0) & ((lo + Bh + rr) < L)
        in_maps.append({
            "x": xh, "Wq": Wq, "Wkv": Wkv, "Wwin": Wwin, "Wout": Wout,
            "qg": qg, "qb": qb, "kg": kg, "bkrow": bkrow, "bwin": bwin,
            "Wr": Wr, "ebias": ebias, "Wg": Wg, "Wu": Wu, "Wd": Wd,
            "Sg": Sg, "Su": Su, "Sd": Sd, "ident": ident, "rel": rel,
            "edgemask": em.reshape(512, 1).copy(),
            "validrow": vr.astype(np.float32),
        })

    if not _NC_CACHE:
        _NC_CACHE.append(build_program())
    nc = _NC_CACHE[0]
    res = run_bass_kernel_spmd(nc, in_maps, core_ids=list(range(NCORES)))
    out = np.empty((Bt, L, E), dtype=np.float32)
    for core in range(NCORES):
        b = core // cores_per_batch
        off = (core % cores_per_batch) * per_core
        out[b, off:off + per_core, :] = np.asarray(res.results[core]["out"])
    return out


# version 38
# speedup vs baseline: 1.1859x; 1.1859x over previous
import contextlib
import numpy as np
import ml_dtypes

import concourse.bass as bass
import concourse.mybir as mybir
from concourse.bass_utils import run_bass_kernel_spmd

F32 = mybir.dt.float32
BF16 = mybir.dt.bfloat16
AF = mybir.ActivationFunctionType
ALU = mybir.AluOpType
AX = mybir.AxisListType
bfd = ml_dtypes.bfloat16

NCORES = 8
TOK = 2048
HT = 1024
NT = 8
NA = 9
E = 1024
NH = 16
D = 64
K17 = 17
SCALE = float(D) ** -0.5
PEN = 10000.0
NEXP = 8
FEXP = 512
EPS = 1e-5
XROWS = 2176
CPT = NH * K17
WIDE = NH * 144


class Op:
    __slots__ = ("i", "eng", "emit", "rd", "wr", "dma", "tag",
                 "deps", "inc", "cnt", "marker", "slot", "prev_target",
                 "strips")

    def __init__(self, i, eng, emit, rd, wr, dma, tag):
        self.i, self.eng, self.emit = i, eng, emit
        self.rd, self.wr, self.dma, self.tag = rd, wr, dma, tag
        self.deps = []
        self.inc = False
        self.cnt = 0
        self.marker = None
        self.strips = None


def _boxes(aps):
    out = []
    for ap in aps:
        if ap is None or not isinstance(ap, bass.AP):
            continue
        space = str(ap.space)
        name = ap.name
        off = int(ap.offset)
        isz = mybir.dt.size(ap.dtype)
        dims = list(ap.ap)
        if "PSUM" in space.upper() or "Psum" in space or "PSum" in space:
            p_lo = ap.base_partition()
            p_hi = p_lo + dims[0][1]
            pstep = dims[0][0]
            foff = off - p_lo * pstep
            fspan = 1 + sum(abs(s) * (n - 1) for s, n in dims[1:])
            b_lo = (foff * isz // 2048) * 2048
            b_hi = ((foff + fspan) * isz + 2047) // 2048 * 2048
            out.append((name, 0, 128, b_lo, b_hi))
        elif "SB" in space.upper():
            p_lo = ap.base_partition()
            p_hi = p_lo + dims[0][1]
            pstep = dims[0][0]
            foff = off - p_lo * pstep
            fspan = 1 + sum(abs(s) * (n - 1) for s, n in dims[1:])
            out.append((name, p_lo, p_hi, foff * isz, (foff + fspan) * isz))
        else:
            span = 1 + sum(abs(s) * (n - 1) for s, n in dims)
            out.append((name, 0, 128, off * isz, (off + span) * isz))
    return out


def _overlap(a, b):
    return (a[1] < b[2] and b[1] < a[2] and a[3] < b[4] and b[3] < a[4])


class Prog:
    def __init__(self):
        self.ops = []
        self.w_by_name = {}
        self.r_by_name = {}

    def op(self, eng, emit, rd=(), wr=(), dma=False, tag="", strips=None):
        o = Op(len(self.ops), eng, emit, _boxes(rd), _boxes(wr), dma, tag)
        o.strips = strips
        deps = set()
        for box in o.rd:
            for wbox, wi in self.w_by_name.get(box[0], ()):
                if _overlap(box, wbox):
                    deps.add(wi)
        for box in o.wr:
            for wbox, wi in self.w_by_name.get(box[0], ()):
                if _overlap(box, wbox):
                    deps.add(wi)
            for rbox, ri in self.r_by_name.get(box[0], ()):
                if _overlap(box, rbox):
                    deps.add(ri)
        o.deps = sorted(deps)
        self.ops.append(o)
        for box in o.wr:
            self.w_by_name.setdefault(box[0], []).append((box, o.i))
        for box in o.rd:
            self.r_by_name.setdefault(box[0], []).append((box, o.i))
        return o

    NSLOT = 8

    def build(self, block, sems):
        import os as _os
        ops = self.ops
        if _os.environ.get("KSERIAL"):
            for o in ops:
                if o.i > 0:
                    o.deps = [o.i - 1]

        rr = {}
        slot_target = {}
        for o in ops:
            if o.dma:
                q = "q_" + o.eng
                s = rr.get(q, 0)
                rr[q] = (s + 1) % Prog.NSLOT
                o.slot = f"{q}_{s}"
                o.prev_target = slot_target.get(o.slot, 0)
                slot_target[o.slot] = o.prev_target + 16
                o.cnt = slot_target[o.slot]

        def chan(o):
            return o.slot if o.dma else o.eng

        def need_edge(p, o):
            if chan(p) != chan(o):
                return True
            if p.dma or o.dma:
                return True
            if o.eng != "tensor":
                return True
            a, b = p.strips, o.strips
            if a is None or b is None:
                return True
            return a[1] <= b[0] or b[1] <= a[0]

        for o in ops:
            if o.dma:
                o.inc = True
            for d in o.deps:
                p = ops[d]
                if need_edge(p, o):
                    p.inc = True
        counts = {}
        for o in ops:
            if o.dma:
                continue
            c = o.eng
            if o.inc:
                counts[c] = counts.get(c, 0) + 1
                o.cnt = counts[c]
        nxt = {}
        for o in reversed(ops):
            c = chan(o)
            if o.inc:
                nxt[c] = o
            o.marker = nxt.get(c)
        totals = dict(counts)
        totals.update(slot_target)

        def runner(eng_name):
            floors = {}

            def _run(eng):
                for o in ops:
                    if o.eng != eng_name:
                        continue
                    need = {}
                    for d in o.deps:
                        p = ops[d]
                        pc = chan(p)
                        if not need_edge(p, o):
                            continue
                        m = p if p.inc else p.marker
                        assert m is not None, f"no marker for op {p.i} ({p.tag})"
                        v = m.cnt
                        if need.get(pc, 0) < v:
                            need[pc] = v
                    if o.dma and o.prev_target > 0:
                        if need.get(o.slot, 0) < o.prev_target:
                            need[o.slot] = o.prev_target
                    for pc in sorted(need):
                        v = need[pc]
                        if floors.get(pc, 0) >= v:
                            continue
                        eng.wait_ge(sems[pc], v)
                        floors[pc] = v
                    ins = o.emit(eng)
                    if o.inc:
                        ins.then_inc(sems[chan(o)], 16 if o.dma else 1)
                if eng_name == "sync":
                    for c in sorted(totals):
                        if c != "sync" and floors.get(c, 0) < totals[c]:
                            eng.wait_ge(sems[c], totals[c])
            return _run

        block.sync(runner("sync"))
        block.tensor(runner("tensor"))
        block.vector(runner("vector"))
        block.scalar(runner("scalar"))
        block.gpsimd(runner("gpsimd"))


def build_program():
    nc = bass.Bass("TRN2", target_bir_lowering=False)

    def din(name, shape, dt=F32):
        return nc.dram_tensor(name, list(shape), dt, kind="ExternalInput").ap()

    x_hbm = din("x", [XROWS, E])
    Wq_h = din("Wq", [E, E])
    Wkv_h = din("Wkv", [E, 2 * E])
    Wwin_h = din("Wwin", [E, 2 * NH])
    Wout_h = din("Wout", [E, E])
    qg_h = din("qg", [128, 1])
    qb_h = din("qb", [128, 1])
    kg_h = din("kg", [128, 1])
    bkrow_h = din("bkrow", [1, 128])
    bwin_h = din("bwin", [1, 2 * NH])
    Wr_h = din("Wr", [E, NEXP])
    ebias_h = din("ebias", [1, NEXP])
    Wg_h = din("Wg", [NEXP, E, FEXP])
    Wu_h = din("Wu", [NEXP, E, FEXP])
    Wd_h = din("Wd", [NEXP, FEXP, E])
    Sg_h = din("Sg", [E, E])
    Su_h = din("Su", [E, E])
    Sd_h = din("Sd", [E, E])
    ident_h = din("ident", [128, 128], BF16)
    rel_h = din("rel", [128, CPT])
    mask_h = din("edgemask", [2 * 2 * 128, 1])
    valid_h = din("validrow", [2, 1152])
    out_hbm = nc.dram_tensor("out", [TOK, E], F32, kind="ExternalOutput").ap()

    sc_dram = [nc.dram_tensor(f"scdram{j}", [128, WIDE], F32, kind="Internal").ap()
               for j in range(NT)]
    aw_dram = [nc.dram_tensor(f"awdram{j}", [128, WIDE], BF16, kind="Internal").ap()
               for j in range(NT)]

    with contextlib.ExitStack() as st:
        def sb(name, shape, dt):
            return st.enter_context(nc.sbuf_tensor(name, shape, dt))

        ident = sb("sb_ident", [128, 128], BF16)
        rel = sb("sb_rel", [128, CPT], F32)
        consts = sb("consts", [128, 8], F32)
        onesrow = sb("onesrow", [1, 128], F32)
        biasrows = sb("biasrows", [1, 64], F32)
        bkrow = sb("sb_bkrow", [1, 128], F32)
        validrow = sb("sb_validrow", [1, 1152], F32)
        gb = sb("gb", [128, 8], F32)
        wrf = sb("wrf", [128, 64], F32)
        wrhi = sb("wrhi", [128, 64], BF16)
        wrlo = sb("wrlo", [128, 64], BF16)
        bmask = sb("bmask", [128, 1152], BF16)

        R1 = sb("R1", [128, 9216], BF16)
        R2 = sb("R2", [128, 9216], BF16)
        R3 = sb("R3", [128, 9216], BF16)
        qpark = sb("qpark", [128, 8192], BF16)
        kpark = sb("kpark", [128, 9216], BF16)
        xtm = sb("xtm", [128, 8192], F32)
        wpool = sb("wpool", [128, 16640], BF16)
        swide = sb("swide", [128, WIDE], F32)
        zb = sb("zb", [128, NT * CPT], F32)
        qfm = [sb(f"qfm{j}", [128, 1024], BF16) for j in range(2)]
        bscr = [sb(f"bscr{j}", [128, 1024], BF16) for j in range(3)]
        fscr = [sb(f"fscr{j}", [128, 1024], F32) for j in range(2)]
        wp_all = sb("wp_all", [128, NT * 2 * NH], F32)
        stq = sb("stq", [128, NT * NH * 2], F32)
        stk = sb("stk", [128, NA * NH * 2], F32)
        rstq = sb("rstq", [128, NT * NH], F32)
        rstk = sb("rstk", [128, NA * NH], F32)
        mu2k = sb("mu2k", [128, NA * NH], F32)
        ccg = [sb(f"ccg{j}", [128, CPT], F32) for j in range(2)]
        abfc = [sb(f"abfc{j}", [128, CPT], BF16) for j in range(3)]
        u1st = sb("u1st", [128, 6 * NT], F32)
        lgall = sb("lgall", [128, NT * NEXP], F32)
        rwall = sb("rwall", [128, NT * NEXP], F32)
        sm8 = sb("sm8", [128, 80], F32)
        sm8b = sb("sm8b", [128, 16], F32)

        ps_A = st.enter_context(nc.psum_tensor("ps_A", [128, 1536], F32))
        ps_B = st.enter_context(nc.psum_tensor("ps_B", [128, 1536], F32))
        ps_t0 = st.enter_context(nc.psum_tensor("ps_t0", [128, 1024], BF16))
        ps_t1 = st.enter_context(nc.psum_tensor("ps_t1", [128, 1024], BF16))

        sems = {}
        for nm in ("sync", "tensor", "vector", "scalar", "gpsimd"):
            sems[nm] = st.enter_context(nc.semaphore("s_" + nm))
        for q in ("q_sync", "q_scalar", "q_gpsimd"):
            for s in range(Prog.NSLOT):
                nm = f"{q}_{s}"
                sems[nm] = st.enter_context(nc.semaphore("s_" + nm))
        block = st.enter_context(nc.Block())

        P = Prog()
        awide = swide[:].bitcast(BF16)[:, 0:WIDE]
        awideP = None

        import os as _os
        _LIM = float(_os.environ.get("KPHASE", "inf"))
        _HLIM = int(_os.environ.get("KHALF", "2"))

        class _Stop(Exception):
            pass

        def ck(x):
            if x > _LIM:
                raise _Stop()
        _S3 = float(_os.environ.get("KS3", "9"))

        def dma(eng, dst, src, tag=""):
            P.op(eng, lambda e, d=dst, s=src: e.dma_start(out=d, in_=s),
                 rd=[src], wr=[dst], dma=True, tag=tag)

        def mm(dst, lhsT, rhs, start, stop, tag=""):
            b = lhsT.base_partition()
            strips = (b // 32, (b + lhsT.partition_size() + 31) // 32)
            P.op("tensor",
                 lambda e, d=dst, l=lhsT, r=rhs, s0=start, s1=stop:
                     e.matmul(d, l, r, start=s0, stop=s1),
                 rd=[lhsT, rhs], wr=[dst], tag=tag, strips=strips)

        def tr(dst, src, tag=""):
            b = src.base_partition()
            strips = (b // 32, (b + src.partition_size() + 31) // 32)
            P.op("tensor", lambda e, d=dst, s=src: e.transpose(d, s, ident[:]),
                 rd=[src, ident[:]], wr=[dst], tag=tag, strips=strips)

        def act(out, in_, func, bias=0.0, scale=1.0, accum=None, eng="scalar",
                tag=""):
            r = [in_] + [x for x in (bias, scale) if isinstance(x, bass.AP)]
            w = [out] + ([accum] if accum is not None else [])
            P.op(eng,
                 lambda e, o=out, i=in_, f=func, b=bias, s=scale, a=accum:
                     e.activation(o, i, f, bias=b, scale=s, accum_out=a),
                 rd=r, wr=w, tag=tag)

        def scopy(out, in_, tag=""):
            P.op("scalar", lambda e, o=out, i=in_: e.copy(o, i),
                 rd=[in_], wr=[out], tag=tag)

        def vop(meth, out, ins, args=(), eng="vector", accum=None, tag=""):
            w = [out] + ([accum] if accum is not None else [])
            def emit(e, m=meth, o=out, ii=tuple(ins), aa=tuple(args), ac=accum):
                f = getattr(e, m)
                if ac is not None:
                    return f(o, *ii, *aa, accum_out=ac)
                return f(o, *ii, *aa)
            P.op(eng, emit, rd=[i for i in ins if isinstance(i, bass.AP)],
                 wr=w, tag=tag)

        def memset(apx, val, eng="vector", tag=""):
            P.op(eng, lambda e, a=apx, v=val: e.memset(a, v), wr=[apx], tag=tag)

        from concourse.hw_specs import get_activation_tables
        _tables = list(get_activation_tables(nc.m.arch).values())

        def _set_id(funcs):
            for idx, fs in enumerate(_tables):
                if all(f in fs for f in funcs):
                    return idx
            raise ValueError(f"no table set with {funcs}")

        def table(*funcs):
            sid = _set_id(funcs)

            def _emit(e, s=sid):
                e.drain()
                e.add_instruction(
                    mybir.InstLoadActFuncSet(
                        name=nc.get_next_instruction_name(),
                        act_func_set_id=s, ins=[], outs=[]))
                return e.drain()
            P.op("scalar", _emit, tag="tbl")

        def w_chunkmaj(hbm, rows, cols, row0=0):
            return bass.AP(hbm.tensor, row0 * cols,
                           [[cols, 128], [128 * cols, rows // 128], [1, cols]])

        def r3d(t, base, n, stride, inner):
            fs = t[:].ap[0][0]
            return bass.AP(t[:].tensor, base, [[fs, 128], [stride, n], [1, inner]])

        memset(onesrow[:], 1.0)
        memset(consts[:], 0.0)
        memset(consts[:, 0:1], EPS)
        dma("sync", biasrows[0:1, 0:32], bwin_h)
        dma("sync", biasrows[0:1, 32:40], ebias_h)
        dma("sync", bkrow[:], bkrow_h)
        dma("sync", ident[:], ident_h)
        dma("sync", rel[:], rel_h)
        dma("sync", wrf[:], w_chunkmaj(Wr_h, E, NEXP))
        vop("tensor_copy", wrhi[:], [wrf[:]])
        vop("tensor_tensor", wrlo[:], [wrf[:], wrhi[:]], (ALU.subtract,))
        memset(awide, 0.0)
        for j in range(NT):
            dma("sync", aw_dram[j], awide)

        WQ = wpool[:, 0:8192]
        WWIN = wpool[:, 8192:8448]
        WKVK = wpool[:, 8448:16640]
        WKVV = wpool[:, 0:8192]
        WOUT = wpool[:, 0:8192]

        gq_c, qbS_c, kg_c = gb[:, 0:1], gb[:, 6:7], gb[:, 2:3]
        mlo_c, mhi_c = gb[:, 4:5], gb[:, 5:6]

        for half in range(_HLIM):
          try:
            B = HT * half
            hx = f"h{half}"

            dma("sync", gb[:, 0:1], qg_h)
            dma("sync", gb[:, 1:2], qb_h)
            dma("sync", gb[:, 2:3], kg_h)
            dma("sync", gb[:, 4:5],
                bass.AP(mask_h.tensor, (half * 2 + 0) * 128, [[1, 128], [1, 1]]))
            dma("sync", gb[:, 5:6],
                bass.AP(mask_h.tensor, (half * 2 + 1) * 128, [[1, 128], [1, 1]]))
            dma("sync", validrow[:],
                bass.AP(valid_h.tensor, half * 1152, [[1152, 1], [1, 1152]]))
            vop("tensor_scalar", gb[:, 6:7], [gb[:, 1:2], SCALE, None], (ALU.mult,))

            dma("scalar", xtm[:].rearrange("p (i n) -> p i n", i=NT),
                bass.AP(x_hbm.tensor, (8 + B) * E,
                        [[E, 128], [128 * E, NT], [1, E]]), tag=f"xin{hx}")

            dma("gpsimd", WQ.rearrange("p (c n) -> p c n", c=8),
                w_chunkmaj(Wq_h, E, E), tag=f"Wq{hx}")
            dma("gpsimd", WWIN.rearrange("p (c n) -> p c n", c=8),
                w_chunkmaj(Wwin_h, E, 2 * NH), tag=f"Wwin{hx}")
            dma("gpsimd", WKVK.rearrange("p (c n) -> p c n", c=8),
                bass.AP(Wkv_h.tensor, 0,
                        [[2 * E, 128], [128 * 2 * E, 8], [1, E]]),
                tag=f"Wkvk{hx}")

            for j in range(3):
                w = 512 if j < 2 else 128
                pg = ps_A[:, 0:512]
                mm(pg[:, 0:w], bkrow[:], validrow[0:1, 512 * j:512 * j + w],
                   True, True, tag=f"bmask{hx}")
                vop("tensor_copy", bmask[:, 512 * j:512 * j + w], [pg[:, 0:w]],
                    tag=f"bmaske{hx}")

            ck(0)
            for a in range(NA):
                xb = bscr[a % 2]
                dma("gpsimd", xb[:],
                    bass.AP(x_hbm.tensor, (B + 128 * a) * E, [[E, 128], [1, E]]),
                    tag=f"xbf{hx}")
                pst = (ps_t0, ps_t1)[a % 2]
                for c in range(8):
                    tr(pst[:, 128 * c:128 * c + 128], xb[:, 128 * c:128 * c + 128],
                       tag=f"xT{hx}")
                scopy(r3d(R1, 128 * a, 8, 1152, 128), pst[:], tag=f"xfm{hx}")

            ck(1)
            table(AF.Silu)
            g = [ps_A[:, 0:512], ps_A[:, 512:1024], ps_A[:, 1024:1536],
                 ps_B[:, 0:512], ps_B[:, 512:1024], ps_B[:, 1024:1536]]

            def xfm(c, col, width=128):
                return R1[:, 1152 * c + col: 1152 * c + col + width]

            for i in range(NT):
                qp = fscr[i % 2]
                for gi in range(2):
                    pg = g[(2 * i + gi) % 4]
                    for c in range(8):
                        mm(pg, xfm(c, 8 + 128 * i),
                           WQ[:, 1024 * c + 512 * gi:1024 * c + 512 * gi + 512],
                           c == 0, c == 7, tag=f"qmm{hx}")
                    act(qp[:, 512 * gi:512 * gi + 512], pg, AF.Silu, tag=f"qev{hx}")
                vop("tensor_copy", qpark[:, 1024 * i:1024 * (i + 1)], [qp[:]],
                    eng="gpsimd", tag=f"qpark{hx}")
                vop("reduce_sum", stq[:, 32 * i:32 * i + 16],
                    [qp[:].rearrange("p (h d) -> p h d", h=NH)], (AX.X,),
                    tag=f"qsum{hx}")
                sq = qfm[0]
                act(sq[:], qp[:], AF.Square, tag=f"qsq{hx}")
                vop("reduce_sum", stq[:, 32 * i + 16:32 * i + 32],
                    [sq[:].rearrange("p (h d) -> p h d", h=NH)], (AX.X,),
                    tag=f"qsqs{hx}")
                pw = g[4 + (i % 2)]
                for c in range(8):
                    mm(pw[:, 0:32], xfm(c, 8 + 128 * i),
                       WWIN[:, 32 * c:32 * c + 32], c == 0, False, tag=f"wpmm{hx}")
                mm(pw[:, 0:32], onesrow[:], biasrows[0:1, 0:32], False, True,
                   tag=f"wpb{hx}")
                act(wp_all[:, 32 * i:32 * i + 32], pw[:, 0:32], AF.Silu,
                    tag=f"wpev{hx}")

            dma("gpsimd", WKVV.rearrange("p (c n) -> p c n", c=8),
                bass.AP(Wkv_h.tensor, E, [[2 * E, 128], [128 * 2 * E, 8], [1, E]]),
                tag=f"Wkvv{hx}")
            for a in range(NA):
                kp = fscr[a % 2]
                for gi in range(2):
                    pg = g[(2 * a + gi) % 4]
                    for c in range(8):
                        mm(pg, xfm(c, 128 * a),
                           WKVK[:, 1024 * c + 512 * gi:1024 * c + 512 * gi + 512],
                           c == 0, c == 7, tag=f"kmm{hx}")
                    act(kp[:, 512 * gi:512 * gi + 512], pg, AF.Silu, tag=f"kev{hx}")
                vop("tensor_copy", kpark[:, 1024 * a:1024 * (a + 1)], [kp[:]],
                    eng="gpsimd", tag=f"kpark{hx}")
                vop("reduce_sum", stk[:, 32 * a:32 * a + 16],
                    [kp[:].rearrange("p (h d) -> p h d", h=NH)], (AX.X,),
                    tag=f"ksum{hx}")
                sq = qfm[0]
                act(sq[:], kp[:], AF.Square, tag=f"ksq{hx}")
                vop("reduce_sum", stk[:, 32 * a + 16:32 * a + 32],
                    [sq[:].rearrange("p (h d) -> p h d", h=NH)], (AX.X,),
                    tag=f"ksqs{hx}")
            for a in range(NA):
                for gi in range(2):
                    pg = g[(2 * a + gi) % 4]
                    for c in range(8):
                        mm(pg, xfm(c, 128 * a),
                           WKVV[:, 1024 * c + 512 * gi:1024 * c + 512 * gi + 512],
                           c == 0, c == 7, tag=f"vmm{hx}")
                    act(R3[:, 1024 * a + 512 * gi:1024 * a + 512 * gi + 512],
                        pg, AF.Silu, tag=f"vev{hx}")

            ck(2)
            table(AF.Ln, AF.Exp)
            for (stx, rst, mu2, n, scl) in ((stq, rstq, mu2k, NT, SCALE),
                                            (stk, rstk, mu2k, NA, 1.0)):
                su = r3d(stx, 0, n, 32, 16)
                sq_ = r3d(stx, 16, n, 32, 16)
                vop("tensor_scalar", su, [su, 1.0 / D, None], (ALU.mult,))
                vop("tensor_scalar", sq_, [sq_, 1.0 / D, None], (ALU.mult,))
                m3 = mu2[:, 0:n * NH].rearrange("p (n h) -> p n h", n=n)
                vop("tensor_tensor", m3, [su, su], (ALU.mult,))
                r3 = rst[:].rearrange("p (n h) -> p n h", n=n)
                vop("tensor_tensor", r3, [sq_, m3], (ALU.subtract,))
                act(rst[:], rst[:], AF.Ln, bias=consts[:, 0:1])
                act(rst[:], rst[:], AF.Exp, scale=-0.5)
                if scl != 1.0:
                    vop("tensor_scalar", rst[:], [rst[:], scl, None], (ALU.mult,))

            wrec = zb[:, 0:NT * 2 * NH]
            act(wrec, wp_all[:], AF.Exp, scale=-1.0, tag="wsexp")
            vop("tensor_scalar", wrec, [wrec, 1.0, None], (ALU.add,))
            vop("reciprocal", wp_all[:], [wrec])
            wv = r3d(wp_all, 0, NT, 32, 16)
            sv = r3d(wp_all, 16, NT, 32, 16)
            vop("tensor_scalar", wv, [wv, 8.0, 0.5], (ALU.mult, ALU.add))
            vop("tensor_scalar", sv, [sv, 9.5, 0.5], (ALU.mult, ALU.add))

            ck(3)
            for i in range(NT):
                z3 = r3d(zb, CPT * i, NH, K17, K17)
                wbc = wp_all[:, 32 * i:32 * i + 16].unsqueeze(2).broadcast_to(
                    [128, NH, K17])
                sbc = wp_all[:, 32 * i + 16:32 * i + 32].unsqueeze(2).broadcast_to(
                    [128, NH, K17])
                vop("tensor_tensor", z3,
                    [rel[:].rearrange("p (h w) -> p h w", h=NH), wbc],
                    (ALU.subtract,), tag=f"z1{hx}")
                vop("tensor_tensor", z3, [z3, sbc], (ALU.mult,), tag=f"z2{hx}")
            table(AF.Sigmoid)
            act(zb[:], zb[:], AF.Sigmoid, tag=f"B{hx}")
            table(AF.Ln, AF.Exp)
            dma("gpsimd", WOUT.rearrange("p (c n) -> p c n", c=8),
                w_chunkmaj(Wout_h, E, E), tag=f"Wout{hx}")
            ATa = [R1[:, 0:1024], R1[:, 1024:2048]]
            ATb = [R1[0:16, 2048:3072], R1[0:16, 3072:4096]]

            def emit_A(i):
                dma("sync", ccg[i % 2][:].rearrange("p (h w) -> p h w", h=NH),
                    bass.AP(sc_dram[i].tensor, 0,
                            [[2305, 128], [144, NH], [1, K17]]), tag=f"scg{hx}")
                lg = ccg[i % 2]
                vop("scalar_tensor_tensor", lg[:],
                    [zb[:, CPT * i:CPT * (i + 1)], -PEN, lg[:]],
                    (ALU.mult, ALU.add), tag=f"lg{hx}")
                lg3 = lg[:].rearrange("p (h w) -> p h w", h=NH)
                P.op("vector",
                     lambda e, o=sm8[:, 0:16], ii=lg3:
                         e.reduce_max(o, ii, axis=AX.X, negate=True),
                     rd=[lg3], wr=[sm8[:, 0:16]], tag=f"mxn{hx}")
                nmb = sm8[:, 0:16].unsqueeze(2).broadcast_to([128, NH, K17])
                vop("tensor_tensor", lg3, [lg3, nmb], (ALU.add,), tag=f"sub{hx}")
                act(lg[:], lg[:], AF.Exp, tag=f"exp{hx}")
                vop("reduce_sum", sm8[:, 16:32], [lg3], (AX.X,), tag=f"sum{hx}")
                vop("reciprocal", sm8[:, 32:48], [sm8[:, 16:32]], tag=f"rin{hx}")
                rib = sm8[:, 32:48].unsqueeze(2).broadcast_to([128, NH, K17])
                vop("tensor_tensor",
                    abfc[i % 3][:].rearrange("p (h w) -> p h w", h=NH),
                    [lg3, rib], (ALU.mult,), tag=f"anrm{hx}")
                dma("scalar",
                    bass.AP(aw_dram[i].tensor, 0,
                            [[2305, 128], [144, NH], [1, K17]]),
                    abfc[i % 3][:].rearrange("p (h w) -> p h w", h=NH),
                    tag=f"asc{hx}")

            awideP = [R1[:, 6144:6144 + WIDE], kpark[:, 0:WIDE]]

            def emit_B(i):
                aw = awideP[i % 2]
                dma("scalar", aw, aw_dram[i], tag=f"awrb{hx}")
                ao_tm = R1[:, 4096 + 1024 * (i % 2):4096 + 1024 * (i % 2) + 1024]
                for w in range(2):
                    for hh in range(8):
                        s = 8 * w + hh
                        tr(ps_t0[:, 128 * hh:128 * hh + 128],
                           aw[:, 144 * s:144 * s + 128], tag=f"aT{hx}")
                        tr(ps_t1[0:16, 128 * hh:128 * hh + 128],
                           aw[:, 144 * s + 128:144 * s + 144], tag=f"aTb{hx}")
                    scopy(ATa[w], ps_t0[:], tag=f"aTe{hx}")
                    scopy(ATb[w], ps_t1[0:16, :], tag=f"aTbe{hx}")
                    av = (ps_A[:, 0:512], ps_A[:, 512:1024])[w]
                    for hh in range(8):
                        s = 8 * w + hh
                        mm(av[:, 64 * hh:64 * hh + 64],
                           ATa[w][:, 128 * hh:128 * hh + 128],
                           R3[:, 1024 * i + 64 * s:1024 * i + 64 * s + 64],
                           True, False, tag=f"av{hx}")
                        mm(av[:, 64 * hh:64 * hh + 64],
                           ATb[w][0:16, 128 * hh:128 * hh + 128],
                           R3[0:16, 1024 * (i + 1) + 64 * s:
                              1024 * (i + 1) + 64 * s + 64],
                           False, True, tag=f"avb{hx}")
                    scopy(ao_tm[:, 512 * w:512 * w + 512], av, tag=f"aoev{hx}")
                for c in range(8):
                    tr(ps_t0[:, 128 * c:128 * c + 128],
                       ao_tm[:, 128 * c:128 * c + 128], tag=f"aoT{hx}")
                scopy(qpark[:, 1024 * i:1024 * (i + 1)], ps_t0[:],
                      tag=f"aofm{hx}")

            def knorm(a):
                kt = bscr[2]
                kt3 = kt[:].rearrange("p (h d) -> p h d", h=NH)
                kp3 = bass.AP(kpark[:].tensor, 1024 * a,
                              [[9216, 128], [64, NH], [1, D]])
                mub = stk[:, 32 * a:32 * a + 16].unsqueeze(2).broadcast_to(
                    [128, NH, D])
                rsb = rstk[:, 16 * a:16 * a + 16].unsqueeze(2).broadcast_to(
                    [128, NH, D])
                vop("tensor_tensor", kt3, [kp3, mub], (ALU.subtract,),
                    tag=f"knrm{hx}")
                vop("tensor_tensor", kt3, [kt3, rsb], (ALU.mult,),
                    tag=f"knrm2{hx}")
                if a == 0:
                    vop("tensor_scalar", kt[:], [kt[:], mlo_c, None], (ALU.mult,),
                        tag=f"kmask{hx}")
                if a == NA - 1:
                    vop("tensor_scalar", kt[:], [kt[:], mhi_c, None], (ALU.mult,),
                        tag=f"kmask{hx}")
                pst = (ps_t0, ps_t1)[a % 2]
                for c in range(8):
                    tr(pst[:, 128 * c:128 * c + 128], kt[:, 128 * c:128 * c + 128],
                       tag=f"kT{hx}")
                act(r3d(R2, 128 * a, 8, 1152, 128), pst[:], AF.Copy,
                    scale=kg_c, tag=f"kfm{hx}")

            def qnorm(i):
                qt = bscr[i % 2]
                qt3 = qt[:].rearrange("p (h d) -> p h d", h=NH)
                qp3 = bass.AP(qpark[:].tensor, 1024 * i,
                              [[8192, 128], [64, NH], [1, D]])
                mub = stq[:, 32 * i:32 * i + 16].unsqueeze(2).broadcast_to(
                    [128, NH, D])
                rsb = rstq[:, 16 * i:16 * i + 16].unsqueeze(2).broadcast_to(
                    [128, NH, D])
                vop("tensor_tensor", qt3, [qp3, mub], (ALU.subtract,),
                    tag=f"qnrm{hx}")
                vop("tensor_tensor", qt3, [qt3, rsb], (ALU.mult,),
                    tag=f"qnrm2{hx}")

            knorm(0)
            knorm(1)
            qnorm(0)
            for i in range(NT):
                if i + 2 <= NT:
                    knorm(i + 2)
                if i + 1 < NT:
                    qnorm(i + 1)
                qt = bscr[i % 2]
                pst = (ps_t0, ps_t1)[i % 2]
                for c in range(8):
                    tr(pst[:, 128 * c:128 * c + 128], qt[:, 128 * c:128 * c + 128],
                       tag=f"qT{hx}")
                vop("tensor_scalar", qfm[i % 2][:], [pst[:], gq_c, qbS_c],
                    (ALU.mult, ALU.add), tag=f"qfme{hx}")

                if _S3 < 2:
                    continue
                for s in [x for x in range(NH) if x % 2 == 0] + \
                         [x for x in range(NH) if x % 2 == 1]:
                    if s < 9:
                        dst = ps_A[:, 512 * (s // 3) + 144 * (s % 3):
                                   512 * (s // 3) + 144 * (s % 3) + 144]
                    else:
                        j = s - 9
                        dst = ps_B[:, 512 * (j // 3) + 144 * (j % 3):
                                   512 * (j // 3) + 144 * (j % 3) + 144]
                    lhs = qfm[i % 2][64 * (s % 2):64 * (s % 2) + 64,
                                     128 * (s // 2):128 * (s // 2) + 128]
                    rhs = R2[64 * (s % 2):64 * (s % 2) + 64,
                             1152 * (s // 2) + 128 * i:1152 * (s // 2) + 128 * i + 144]
                    mm(dst, lhs, rhs, True, False, tag=f"sc{hx}")
                    rhsb = bmask[64 * (s % 2):64 * (s % 2) + 64,
                                 128 * i:128 * i + 144]
                    mm(dst, lhs, rhsb, False, True, tag=f"scb{hx}")
                if _S3 < 3:
                    continue
                fs_sw = swide[:].ap[0][0]
                srcA = bass.AP(ps_A[:].tensor, int(ps_A[:].offset),
                               [[1536, 128], [512, 3], [144, 3], [1, 144]])
                dstA = bass.AP(swide[:].tensor, 0, [[fs_sw, 128], [432, 3],
                                                    [144, 3], [1, 144]])
                scopy(dstA, srcA, tag=f"scevA{hx}")
                dma("sync", bass.AP(sc_dram[i].tensor, 0,
                                    [[2304, 128], [1, 1296]]),
                    swide[:, 0:1296], tag=f"scoutA{hx}")
                srcB1 = bass.AP(ps_B[:].tensor, int(ps_B[:].offset),
                                [[1536, 128], [512, 2], [144, 3], [1, 144]])
                dstB1 = bass.AP(swide[:].tensor, 1296, [[fs_sw, 128], [432, 2],
                                                        [144, 3], [1, 144]])
                scopy(dstB1, srcB1, tag=f"scevB{hx}")
                srcB2 = bass.AP(ps_B[:].tensor, int(ps_B[:].offset) + 1024,
                                [[1536, 128], [1, 144]])
                scopy(swide[:, 2160:2304], srcB2, tag=f"scevB2{hx}")
                dma("sync", bass.AP(sc_dram[i].tensor, 1296,
                                     [[2304, 128], [1, 1008]]),
                    swide[:, 1296:2304], tag=f"scoutB{hx}")
                if i >= 2:
                    emit_B(i - 2)
                emit_A(i)
            emit_B(NT - 2)
            emit_B(NT - 1)

            ck(6)
            table(AF.Silu)
            def s6_tail(i):
                ws_o = fscr[i % 2]
                for gi in range(2):
                    pg = g[(2 * i + gi) % 4]
                    act(ws_o[:, 512 * gi:512 * gi + 512], pg, AF.Silu,
                        tag=f"woev{hx}")
                xsl = xtm[:, 1024 * i:1024 * (i + 1)]
                vop("scalar_tensor_tensor", xsl, [ws_o[:], 1.0, xsl],
                    (ALU.bypass, ALU.add), accum=u1st[:, i:i + 1], tag=f"u1{hx}")
                act(qfm[0][:], xsl, AF.Square, accum=u1st[:, NT + i:NT + i + 1],
                    tag=f"u1sq{hx}")

            for i in range(NT):
                ao_fm = qpark[:, 1024 * i:1024 * (i + 1)]
                for gi in range(2):
                    pg = g[(2 * i + gi) % 4]
                    for c in range(8):
                        mm(pg, ao_fm[:, 128 * c:128 * c + 128],
                           WOUT[:, 1024 * c + 512 * gi:1024 * c + 512 * gi + 512],
                           c == 0, c == 7, tag=f"wo{hx}")
                if i >= 1:
                    s6_tail(i - 1)
            s6_tail(NT - 1)

            ck(7)
            table(AF.Ln, AF.Exp)
            mu_u = u1st[:, 0:NT]
            sq_u = u1st[:, NT:2 * NT]
            m2_u = u1st[:, 2 * NT:3 * NT]
            rs_u = u1st[:, 3 * NT:4 * NT]
            vop("tensor_scalar", mu_u, [mu_u, 1.0 / E, None], (ALU.mult,))
            vop("tensor_scalar", sq_u, [sq_u, 1.0 / E, None], (ALU.mult,))
            vop("tensor_tensor", m2_u, [mu_u, mu_u], (ALU.mult,))
            vop("tensor_tensor", rs_u, [sq_u, m2_u], (ALU.subtract,))
            act(rs_u, rs_u, AF.Ln, bias=consts[:, 0:1])
            act(rs_u, rs_u, AF.Exp, scale=-0.5)
            for i in range(NT):
                xsl = xtm[:, 1024 * i:1024 * (i + 1)]
                vop("tensor_scalar", xsl,
                    [xsl, mu_u[:, i:i + 1], rs_u[:, i:i + 1]],
                    (ALU.subtract, ALU.mult), tag=f"x1{hx}")
            for i in range(NT):
                xsl = xtm[:, 1024 * i:1024 * (i + 1)]
                x1b = bscr[i % 2]
                vop("tensor_copy", x1b[:], [xsl], eng="gpsimd", tag=f"x1b{hx}")
                pst = (ps_t0, ps_t1)[i % 2]
                for c in range(8):
                    tr(pst[:, 128 * c:128 * c + 128], x1b[:, 128 * c:128 * c + 128],
                       tag=f"x1T{hx}")
                scopy(r3d(R3, 128 * i, 8, 1024, 128), pst[:], tag=f"x1fm{hx}")
                lob = bscr[2]
                vop("tensor_tensor", lob[:], [xsl, x1b[:]], (ALU.subtract,),
                    tag=f"lo{hx}")
                for c in range(8):
                    tr(pst[:, 128 * c:128 * c + 128], lob[:, 128 * c:128 * c + 128],
                       tag=f"loT{hx}")
                scopy(qfm[i % 2][:], pst[:], tag=f"lofm{hx}")
                pg = g[i % 2]
                for c in range(8):
                    mm(pg[:, 0:8], R3[:, 1024 * c + 128 * i:1024 * c + 128 * i + 128],
                       wrhi[:, 8 * c:8 * c + 8], c == 0, False, tag=f"rt1{hx}")
                for c in range(8):
                    mm(pg[:, 0:8], R3[:, 1024 * c + 128 * i:1024 * c + 128 * i + 128],
                       wrlo[:, 8 * c:8 * c + 8], False, False, tag=f"rt2{hx}")
                for c in range(8):
                    mm(pg[:, 0:8], qfm[i % 2][:, 128 * c:128 * c + 128],
                       wrhi[:, 8 * c:8 * c + 8], False, False, tag=f"rt3{hx}")
                mm(pg[:, 0:8], onesrow[:], biasrows[0:1, 32:40], False, True,
                   tag=f"rtb{hx}")
                vop("tensor_copy", lgall[:, 8 * i:8 * i + 8], [pg[:, 0:8]],
                    tag=f"lgev{hx}")

            ck(8)
            table(AF.Silu)
            lg3a = lgall[:].rearrange("p (i e) -> p i e", i=NT)
            vop("reduce_max", sm8[:, 0:8], [lg3a], (AX.X,), tag=f"rm1{hx}")
            m1b = sm8[:, 0:8].unsqueeze(2).broadcast_to([128, NT, NEXP])
            rw3 = rwall[:].rearrange("p (i e) -> p i e", i=NT)
            vop("tensor_tensor", rw3, [lg3a, m1b], (ALU.is_equal,), tag=f"oh1{hx}")
            msk3 = sm8[:, 16:80].rearrange("p (i e) -> p i e", i=NT)
            vop("scalar_tensor_tensor", msk3, [rw3, -1e30, lg3a],
                (ALU.mult, ALU.add), tag=f"mskd{hx}")
            vop("reduce_max", sm8[:, 8:16], [msk3], (AX.X,), tag=f"rm2{hx}")
            m2b = sm8[:, 8:16].unsqueeze(2).broadcast_to([128, NT, NEXP])
            vop("tensor_tensor", lg3a, [msk3, m2b], (ALU.is_equal,), tag=f"oh2{hx}")
            vop("tensor_tensor", sm8[:, 0:8], [sm8[:, 8:16], sm8[:, 0:8]],
                (ALU.subtract,), tag=f"d{hx}")
            vop("tensor_scalar", sm8[:, 0:8], [sm8[:, 0:8], -1e-20, None],
                (ALU.min,), tag=f"dg{hx}")
            act(sm8[:, 8:16], sm8[:, 0:8], AF.Silu, tag=f"sild{hx}")
            vop("reciprocal", sm8b[:, 0:8], [sm8[:, 0:8]], tag=f"drec{hx}")
            vop("tensor_tensor", sm8b[:, 8:16], [sm8[:, 8:16], sm8b[:, 0:8]],
                (ALU.mult,), tag=f"p2{hx}")
            vop("tensor_tensor", msk3, [lg3a, rw3], (ALU.subtract,), tag=f"dd{hx}")
            p2b = sm8b[:, 8:16].unsqueeze(2).broadcast_to([128, NT, NEXP])
            vop("tensor_tensor", msk3, [msk3, p2b], (ALU.mult,), tag=f"ddp{hx}")
            vop("tensor_tensor", rw3, [msk3, rw3], (ALU.add,), tag=f"rw{hx}")

            groups = [("sh",), (0, 1), (2, 3), (4, 5), (6, 7)]
            WEXTA = qpark[:, 0:8192]
            WEXTB = kpark[:, 0:8192]

            def gw_regions(gi_):
                if gi_ % 2 == 0:
                    return wpool[:, 0:8192], wpool[:, 8192:16384], wpool[:, 0:8192]
                return WEXTA, WEXTB, WEXTA

            def stream_p1(gi_):
                if gi_ >= len(groups):
                    return
                grp = groups[gi_]
                RA, RB, _ = gw_regions(gi_)
                if grp[0] == "sh":
                    dma("gpsimd", RA.rearrange("p (c n) -> p c n", c=8),
                        w_chunkmaj(Sg_h, E, E), tag=f"Sg{hx}")
                    dma("gpsimd", RB.rearrange("p (c n) -> p c n", c=8),
                        w_chunkmaj(Su_h, E, E), tag=f"Su{hx}")
                else:
                    for k_, e_ in enumerate(grp):
                        dma("gpsimd",
                            RA[:, 4096 * k_:4096 * (k_ + 1)].rearrange(
                                "p (c n) -> p c n", c=8),
                            bass.AP(Wg_h.tensor, e_ * E * FEXP,
                                    [[FEXP, 128], [128 * FEXP, 8], [1, FEXP]]),
                            tag=f"Wg{e_}{hx}")
                        dma("gpsimd",
                            RB[:, 4096 * k_:4096 * (k_ + 1)].rearrange(
                                "p (c n) -> p c n", c=8),
                            bass.AP(Wu_h.tensor, e_ * E * FEXP,
                                    [[FEXP, 128], [128 * FEXP, 8], [1, FEXP]]),
                            tag=f"Wu{e_}{hx}")

            stream_p1(0)
            for gi_, grp in enumerate(groups):
                RA, RB, RD_ = gw_regions(gi_)

                ng = len(grp)
                Fh = E if grp[0] == "sh" else FEXP
                pairs = [(i, k_, e_) for i in range(NT)
                         for k_, e_ in enumerate(grp)]
                gslots = [g[0], g[1]]
                uslots = [ps_A[:, 1024:1536], ps_B[:, 1024:1536]]
                pcnt = [0]
                rec = {}

                def pair_mms(j):
                    i, k_, e_ = pairs[j]
                    slots = []
                    for fg in range(Fh // 512):
                        pg = gslots[pcnt[0] % 2]
                        pu = uslots[pcnt[0] % 2]
                        pcnt[0] += 1
                        slots.append((pg, pu))
                        for c in range(8):
                            WGsl = RA[:, 1024 * c + 512 * fg:
                                      1024 * c + 512 * fg + 512] \
                                if grp[0] == "sh" else \
                                RA[:, 4096 * k_ + 512 * c:4096 * k_ + 512 * c + 512]
                            mm(pg, R3[:, 1024 * c + 128 * i:
                                      1024 * c + 128 * i + 128],
                               WGsl, c == 0, c == 7, tag=f"gmm{hx}")
                        for c in range(8):
                            WUsl = RB[:, 1024 * c + 512 * fg:
                                      1024 * c + 512 * fg + 512] \
                                if grp[0] == "sh" else \
                                RB[:, 4096 * k_ + 512 * c:4096 * k_ + 512 * c + 512]
                            mm(pu, R3[:, 1024 * c + 128 * i:
                                      1024 * c + 128 * i + 128],
                               WUsl, c == 0, c == 7, tag=f"umm{hx}")
                    rec[j] = slots

                def pair_tail(j):
                    i, k_, e_ = pairs[j]
                    gbuf = fscr[j % 2]
                    hbuf = bscr[j % 2]
                    for fg, (pg, pu) in enumerate(rec.pop(j)):
                        act(gbuf[:, 512 * fg:512 * fg + 512], pg, AF.Silu,
                            tag=f"gev{hx}")
                        if grp[0] == "sh":
                            vop("tensor_tensor", hbuf[:, 512 * fg:512 * fg + 512],
                                [gbuf[:, 512 * fg:512 * fg + 512], pu],
                                (ALU.mult,), tag=f"hsh{hx}")
                        else:
                            vop("scalar_tensor_tensor",
                                hbuf[:, 512 * fg:512 * fg + 512],
                                [pu, rwall[:, 8 * i + e_:8 * i + e_ + 1],
                                 gbuf[:, 512 * fg:512 * fg + 512]],
                                (ALU.mult, ALU.mult), tag=f"hrw{hx}")
                    nch = Fh // 128
                    pst = (ps_t0, ps_t1)[j % 2]
                    for c in range(nch):
                        tr(pst[:, 128 * c:128 * c + 128],
                           hbuf[:, 128 * c:128 * c + 128], tag=f"hT{hx}")
                    base = (i * ng + k_) * (512 if grp[0] != "sh" else 1024)
                    scopy(R2[:, base:base + 128 * nch],
                          pst[:, 0:128 * nch], tag=f"hpk{hx}")

                skew = 0 if grp[0] == "sh" else 1
                for j in range(len(pairs)):
                    pair_mms(j)
                    if j >= skew:
                        pair_tail(j - skew)
                for j in range(len(pairs) - skew, len(pairs)):
                    pair_tail(j)

                stream_p1(gi_ + 1)
                if grp[0] == "sh":
                    dma("gpsimd", RD_.rearrange("p (c n) -> p c n", c=8),
                        w_chunkmaj(Sd_h, E, E), tag=f"Sd{hx}")
                else:
                    for k_, e_ in enumerate(grp):
                        dma("gpsimd",
                            RD_[:, 4096 * k_:4096 * (k_ + 1)].rearrange(
                                "p (c n) -> p c n", c=4),
                            bass.AP(Wd_h.tensor, e_ * FEXP * E,
                                    [[E, 128], [128 * E, 4], [1, E]]),
                            tag=f"Wd{e_}{hx}")

                last_grp = gi_ == len(groups) - 1

                def eo_mms(i):
                    eo = (ps_B[:, 0:1024], ps_A[:, 0:1024])[i % 2]
                    first = True
                    for k_, e_ in enumerate(grp):
                        nch = (E if grp[0] == "sh" else FEXP) // 128
                        base = (i * ng + k_) * (512 if grp[0] != "sh" else 1024)
                        for c in range(nch):
                            for gi2 in range(2):
                                WDsl = RD_[:, 1024 * c + 512 * gi2:
                                           1024 * c + 512 * gi2 + 512] \
                                    if grp[0] == "sh" else \
                                    RD_[:, 4096 * k_ + 1024 * c + 512 * gi2:
                                        4096 * k_ + 1024 * c + 512 * gi2 + 512]
                                mm(eo[:, 512 * gi2:512 * gi2 + 512],
                                   R2[:, base + 128 * c:base + 128 * c + 128],
                                   WDsl, first and c == 0,
                                   (k_ == ng - 1) and c == nch - 1,
                                   tag=f"eo{hx}")
                        first = False

                def eo_tail(i):
                    eo = (ps_B[:, 0:1024], ps_A[:, 0:1024])[i % 2]
                    for gi2 in range(2):
                        xsl = xtm[:, 1024 * i + 512 * gi2:1024 * i + 512 * gi2 + 512]
                        acc_ap = (u1st[:, 4 * NT + i:4 * NT + i + 1] if
                                  (last_grp and gi2 == 0) else
                                  (u1st[:, 5 * NT + i:5 * NT + i + 1] if
                                   (last_grp and gi2 == 1) else None))
                        vop("scalar_tensor_tensor", xsl,
                            [eo[:, 512 * gi2:512 * gi2 + 512], 1.0, xsl],
                            (ALU.bypass, ALU.add), accum=acc_ap, tag=f"eoadd{hx}")
                    if last_grp:
                        act(qfm[0][:], xtm[:, 1024 * i:1024 * (i + 1)], AF.Square,
                            accum=u1st[:, NT + i:NT + i + 1], tag=f"u2sq{hx}")

                for i in range(NT):
                    eo_mms(i)
                    if i >= 1:
                        eo_tail(i - 1)
                eo_tail(NT - 1)

            ck(9)
            table(AF.Ln, AF.Exp)
            vop("tensor_tensor", mu_u, [u1st[:, 4 * NT:5 * NT],
                                        u1st[:, 5 * NT:6 * NT]], (ALU.add,))
            vop("tensor_scalar", mu_u, [mu_u, 1.0 / E, None], (ALU.mult,))
            vop("tensor_scalar", sq_u, [u1st[:, NT:2 * NT], 1.0 / E, None],
                (ALU.mult,))
            vop("tensor_tensor", m2_u, [mu_u, mu_u], (ALU.mult,))
            vop("tensor_tensor", rs_u, [sq_u, m2_u], (ALU.subtract,))
            act(rs_u, rs_u, AF.Ln, bias=consts[:, 0:1])
            act(rs_u, rs_u, AF.Exp, scale=-0.5)
            for i in range(NT):
                xsl = xtm[:, 1024 * i:1024 * (i + 1)]
                vop("tensor_scalar", xsl,
                    [xsl, mu_u[:, i:i + 1], rs_u[:, i:i + 1]],
                    (ALU.subtract, ALU.mult), tag=f"x2{hx}")
                dma("scalar",
                    bass.AP(out_hbm.tensor, (HT * half + 128 * i) * E,
                            [[E, 128], [1, E]]),
                    xsl, tag=f"out{hx}")
          except _Stop:
            pass

        P.build(block, sems)
    return nc


_NC_CACHE = []


def kernel(**inputs):
    x = np.ascontiguousarray(np.asarray(inputs["L0_attn_x"], dtype=np.float32))
    Bt, L, _ = x.shape
    per_core = (Bt * L) // NCORES
    cores_per_batch = L // per_core

    def f32(v):
        return np.ascontiguousarray(np.asarray(v, dtype=np.float32))

    Wq = f32(inputs["L0_attn_Wq"])
    Wkv = f32(inputs["L0_attn_Wkv"])
    Wwin = f32(inputs["L0_attn_Wwin"])
    Wout = f32(inputs["L0_attn_Wout"])
    qg = np.tile(f32(inputs["L0_attn_q_g"]), 2)[:, None].copy()
    qb = np.tile(f32(inputs["L0_attn_q_b"]), 2)[:, None].copy()
    kg = np.tile(f32(inputs["L0_attn_k_g"]), 2)[:, None].copy()
    bkrow = np.tile(f32(inputs["L0_attn_k_b"]), 2)[None, :].copy()
    bwin = f32(inputs["L0_attn_bwin"])[None, :].copy()
    Wr = f32(inputs["L0_ffn_Wr"])
    ebias = f32(inputs["L0_ffn_expert_bias"])[None, :].copy()
    Wg = f32(inputs["L0_ffn_Wg"])
    Wu = f32(inputs["L0_ffn_Wu"])
    Wd = f32(inputs["L0_ffn_Wd"])
    Sg = f32(inputs["L0_ffn_Sg"])
    Su = f32(inputs["L0_ffn_Su"])
    Sd = f32(inputs["L0_ffn_Sd"])

    ident = np.eye(128, dtype=bfd)
    relv = np.abs(np.arange(K17, dtype=np.float32) - (K17 // 2))
    rel = np.broadcast_to(np.tile(relv, NH)[None, :], (128, CPT)).copy()

    in_maps = []
    for core in range(NCORES):
        b = core // cores_per_batch
        off = (core % cores_per_batch) * per_core
        xh = np.zeros((XROWS, E), dtype=np.float32)
        lo = off - 8
        hi = off + per_core + 8
        s_lo = max(lo, 0)
        s_hi = min(hi, L)
        xh[s_lo - lo:s_hi - lo, :] = x[b, s_lo:s_hi, :]
        em = np.zeros((2, 2, 128, 1), dtype=np.float32)
        vr = np.zeros((2, 1152), dtype=np.float32)
        for half in range(2):
            Bh = HT * half
            for p in range(128):
                em[half, 0, p, 0] = 1.0 if 0 <= (lo + Bh + p) < L else 0.0
                em[half, 1, p, 0] = 1.0 if 0 <= (lo + Bh + 1024 + p) < L else 0.0
            rr = np.arange(1152)
            vr[half] = ((lo + Bh + rr) >= 0) & ((lo + Bh + rr) < L)
        in_maps.append({
            "x": xh, "Wq": Wq, "Wkv": Wkv, "Wwin": Wwin, "Wout": Wout,
            "qg": qg, "qb": qb, "kg": kg, "bkrow": bkrow, "bwin": bwin,
            "Wr": Wr, "ebias": ebias, "Wg": Wg, "Wu": Wu, "Wd": Wd,
            "Sg": Sg, "Su": Su, "Sd": Sd, "ident": ident, "rel": rel,
            "edgemask": em.reshape(512, 1).copy(),
            "validrow": vr.astype(np.float32),
        })

    if not _NC_CACHE:
        _NC_CACHE.append(build_program())
    nc = _NC_CACHE[0]
    res = run_bass_kernel_spmd(nc, in_maps, core_ids=list(range(NCORES)))
    out = np.empty((Bt, L, E), dtype=np.float32)
    for core in range(NCORES):
        b = core // cores_per_batch
        off = (core % cores_per_batch) * per_core
        out[b, off:off + per_core, :] = np.asarray(res.results[core]["out"])
    return out
